# version 08 (final)
# baseline (speedup 1.0000x reference)
import numpy as np
from contextlib import ExitStack
import concourse.bass as bass
import concourse.mybir as mybir
from concourse.bass_utils import run_bass_kernel_spmd

F32 = mybir.dt.float32
BF16 = mybir.dt.bfloat16
AF = mybir.ActivationFunctionType
ALU = mybir.AluOpType
AX = mybir.AxisListType.X

D = 2048
DC = 16
TT = 512
SEQ = 2048
NSEQ = 2
TPS = SEQ // TT
CH = 128
NCH = TT // CH
NH = 16
NG = 8
FF = 1408
FFC = 11
NE = 8
EPS = 1e-5
XBCW = 3 + TT

DEBUG_TAPS = []
N_TILES = NSEQ * TPS
LAST_RESULTS = [None]


def plan_blocks():
    blocks = []
    off = 0
    wslices = {}

    def add(w, K, M, mstep=256, mlist=None):
        nonlocal off
        nkc = K // 128
        idxs = []
        if mlist is None:
            mlist = []
            m0 = 0
            while m0 < M:
                mw = min(mstep, M - m0)
                mlist.append((m0, mw))
                m0 += mw
        for (m0, mw) in mlist:
            blocks.append(dict(w=w, kc0=0, nk=nkc, m0=m0, mw=mw, off=off, cols=nkc * mw))
            idxs.append(len(blocks) - 1)
            off += nkc * mw
        wslices.setdefault(w, []).extend(idxs)
        return idxs

    add("gwt", 2048, 8)
    add("w_in", 2048, 6160, mlist=[(m, 256) for m in range(0, 6144, 256)] + [(6144, 16)])
    add("w_out", 2048, 2048)
    add("w1", 2048, 8192)
    add("w3", 2048, 8192)
    w2l = []
    for m in range(0, 2048, 128):
        for kc0 in (0, 32):
            blocks.append(dict(w="w2", kc0=kc0, nk=32, m0=m, mw=128, off=off, cols=32 * 128))
            w2l.append(len(blocks) - 1)
            off += 32 * 128
    wslices["w2"] = w2l
    add("ws1t", 2048, 1408)
    add("ws3t", 2048, 1408)
    add("ws2t", 1408, 2048)
    for e in range(NE):
        add("we1t_%d" % e, 2048, 1408)
        add("we3t_%d" % e, 2048, 1408)
        add("we2t_%d" % e, 1408, 2048)
    return blocks, wslices, off

BLOCKS, WSLICES, SCRATCH_COLS = plan_blocks()
_half = SCRATCH_COLS // 2
WSC_SPLIT = min(blk["off"] for blk in BLOCKS if blk["off"] >= _half)


class Builder:
    def __init__(self, nc, es):
        self.nc = nc
        self.es = es
        self.sems = {}
        self.waited = {}
        self.x_ready = ("sPE", 0)
        self.T = nc.tensor; self.V = nc.vector; self.A = nc.scalar
        self.P = nc.gpsimd; self.S = nc.sync

    def mksem(self, *names):
        for n in names:
            self.sems[n] = [self.es.enter_context(self.nc.semaphore(n)), 0]

    def inc(self, name, instr, k=1):
        s = self.sems[name]
        instr.then_inc(s[0], k)
        s[1] += k
        return s[1]

    def val(self, name):
        return self.sems[name][1]

    def wait(self, eng, name, v=None, force=False):
        if v is None:
            v = self.sems[name][1]
        key = (id(eng), name)
        if force or self.waited.get(key, -1) < v:
            eng.wait_ge(self.sems[name][0], v)
            if self.waited.get(key, -1) < v:
                self.waited[key] = v

    def hz(self, eng):
        name = "sV" if eng is self.V else ("sA" if eng is self.A else "sP")
        key = (id(eng), name)
        self.wait(eng, name, self.waited.get(key, 0), force=True)


def build_nc():
    nc = bass.Bass("TRN2", target_bir_lowering=False)
    es = ExitStack()
    b = Builder(nc, es)
    b.mksem("sPE", "sV", "sA", "sP")
    b.mksem("sW0", "sW1", "sW2", "sW3", "sWS0", "sWS1", "sXI", "sXO", "sPR",
            "sB0", "sB1", "sB2", "sB3", "sSD1", "sSD2", "sCF", "sCB", "sTP")
    T, V, A_, P, S = b.T, b.V, b.A, b.P, b.S

    def vser(instr):
        v = b.inc("sV", instr)
        b.wait(b.V, "sV", v, force=True)
        return v

    def aser(instr):
        v = b.inc("sA", instr)
        b.wait(b.A, "sA", v, force=True)
        return v

    def dma(chan, eng, out, in_):
        b.wait(eng, chan, b.val(chan), force=False)
        return b.inc(chan, eng.dma_start(out=out, in_=in_), 16)

    dr = {}
    dr["xin"] = nc.dram_tensor("xin", [NSEQ, D, SEQ], F32, kind="ExternalInput").ap()
    dr["xout"] = nc.dram_tensor("xout", [NSEQ, D, SEQ], F32, kind="ExternalOutput").ap()
    for n, s in {
        "w_in": [2048, 6160], "w_out": [2048, 2048], "w1": [2048, 8192],
        "w3": [2048, 8192], "w2": [8192, 2048], "ws1t": [2048, 1408],
        "ws3t": [2048, 1408], "ws2t": [1408, 2048], "gwt": [2048, 8],
        "we1t": [NE, 2048, 1408], "we3t": [NE, 2048, 1408], "we2t": [NE, 1408, 2048],
        "ln1wc": [128, 16], "normwc": [128, 16], "ln2wc": [128, 16],
        "cwt": [128, 128], "cb": [128, 32], "dtb": [16, 1], "alog": [16, 1],
        "drow": [1, 16], "biasrow": [1, 8], "identf": [128, 128], "maskadd": [128, 128],
    }.items():
        dr[n] = nc.dram_tensor(n, s, F32, kind="ExternalInput").ap()
    wsc0 = nc.dram_tensor("wsc0", [128, WSC_SPLIT], BF16, kind="Internal").ap()
    wsc1 = nc.dram_tensor("wsc1", [128, SCRATCH_COLS - WSC_SPLIT], BF16, kind="Internal").ap()

    def wsc_view(off, cols):
        if off < WSC_SPLIT:
            assert off + cols <= WSC_SPLIT
            return wsc0[:, off:off + cols]
        return wsc1[:, off - WSC_SPLIT:off - WSC_SPLIT + cols]
    rowsc = nc.dram_tensor("rowsc", [8, 2048], F32, kind="Internal").ap()
    rowscb = nc.dram_tensor("rowscb", [NE, TT], BF16, kind="Internal").ap()

    _uid = [0]

    def uname(s):
        _uid[0] += 1
        return "%s_u%d" % (s, _uid[0])

    def sb(name, shape, dt):
        return es.enter_context(nc.sbuf_tensor(name, shape, dt))

    xA = sb("xA", [128, DC * TT], F32)
    xn = sb("xn", [128, DC * TT], BF16)
    yb = sb("yb", [128, DC * TT], BF16)
    wb = [sb("wb%d" % i, [128, 4096], BF16) for i in range(3)]
    St = sb("St", [128, NH * 128], F32)
    Stb = sb("Stb", [128, NH * 128], BF16)
    identf = sb("identfs", [128, 128], F32)
    identb = sb("identbs", [128, 128], BF16)
    maskadd = sb("maskadds", [128, 128], F32)
    cwt = sb("cwts", [128, 128], F32)
    cbv = sb("cbs", [128, 32], F32)
    lnw = {n: sb(n + "s", [128, 16], F32) for n in ("ln1wc", "normwc", "ln2wc")}
    dtb = sb("dtbs", [16, 1], F32)
    aneg = sb("anegs", [16, 1], F32)
    Dbc = sb("Dbcs", [128, 16], F32)
    biasbc = sb("biasbcs", [128, 8], F32)
    gwt_s = sb("gwts", [128, 128], BF16)
    onesb = sb("onesb", [128, 1], BF16)
    onesf = sb("onesf", [33, 128], F32)
    zeros16 = sb("zeros16", [16, 128], F32)
    rowar = sb("rowar", [128, 2048], F32)
    smf = sb("smf", [128, 96], F32)
    constv = sb("constv", [128, 2], F32)
    stg = [sb("stg%d" % i, [128, 1024], F32) for i in range(2)]
    pcb = [sb("pc%d" % i, [128, 1024], BF16) for i in range(2)]
    nc.const_aps.register(F32, EPS, constv[:, 0:1])
    nc.const_aps.register(F32, 1.0, constv[:, 1:2])
    tailb = sb("tailb", [128, 96], BF16)
    pquad = es.enter_context(nc.psum_tensor("pquad", [128, 2048], F32))
    pq8 = es.enter_context(nc.psum_tensor("pq8", [128, 2048], F32))
    ptp = pq8[:, 0:512].bitcast(BF16)

    def row(i):
        return rowar[0:1, i * TT:(i + 1) * TT]

    taps = {}

    def tap(name, ap, shape, dt=F32):
        if name in DEBUG_TAPS:
            out = nc.dram_tensor("tap_" + name, list(shape), dt, kind="ExternalOutput").ap()
            for sn in ("sV", "sA", "sP", "sPE"):
                b.wait(P, sn)
            tk = dma("sTP", P, out[:, :], ap)
            b.wait(P, "sTP", tk)
            taps[name] = shape

    with nc.Block() as _blk:
        for (dst, src) in [(identf, dr["identf"]), (maskadd, dr["maskadd"]),
                           (cwt, dr["cwt"]), (cbv, dr["cb"]), (dtb, dr["dtb"]),
                           (lnw["ln1wc"], dr["ln1wc"]), (lnw["normwc"], dr["normwc"]),
                           (lnw["ln2wc"], dr["ln2wc"])]:
            dma("sPR", S, dst[:], src[:, :])
        dma("sPR", S, smf[0:16, 80:81], dr["alog"][:, :])
        dma("sPR", S, Dbc[:], dr["drow"][:, :].to_broadcast((128, 16)))
        pv = dma("sPR", S, biasbc[:], dr["biasrow"][:, :].to_broadcast((128, 8)))
        b.wait(V, "sPR", pv)
        V.memset(onesb[:], 1.0)
        V.memset(onesf[0:1, :], 1.0)
        V.memset(onesf[32:33, :], 1.0)
        V.memset(zeros16[:], 0.0)
        V.memset(constv[:, 0:1], EPS)
        V.memset(constv[:, 1:2], 1.0)
        V.tensor_copy(identb[:], identf[:])
        b.wait(A_, "sPR", pv)
        aA = b.inc("sA", A_.activation(smf[0:16, 81:82], smf[0:16, 80:81], AF.Exp))
        b.wait(V, "sA", aA)
        b.inc("sV", V.tensor_scalar(aneg[:], smf[0:16, 81:82], -1.0, None, op0=ALU.mult))

        pstate_pro = dict(i=0, cast_done=[("sV", 0), ("sV", 0)])

        def emit_prologue(idxs):
            SUB = 1024
            for bi in idxs:
                blk = BLOCKS[bi]
                w = blk["w"]
                if w.startswith("we"):
                    base, e = w.rsplit("_", 1)
                    srcw = dr[base][int(e)]
                else:
                    srcw = dr[w]
                kc0, nk, m0, mw, cols = blk["kc0"], blk["nk"], blk["m0"], blk["mw"], blk["cols"]
                rows = srcw[kc0 * 128:(kc0 + nk) * 128, m0:m0 + mw].rearrange(
                    "(k p) m -> p k m", p=128)
                st_tok = None
                j = 0
                while j < cols:
                    wsub = min(SUB, cols - j)
                    assert wsub % mw == 0 or j % mw == 0
                    k0 = j // mw
                    k1 = (j + wsub) // mw
                    sl = pstate_pro["i"] % 2
                    pstate_pro["i"] += 1
                    lch = "sW%d" % sl
                    sch = "sWS%d" % sl
                    cd = pstate_pro["cast_done"][sl]
                    b.wait(S, cd[0], cd[1])
                    tok = dma(lch, S,
                              stg[sl][:, 0:wsub].rearrange("p (k m) -> p k m", m=mw),
                              rows[:, k0:k1, :])
                    eng, semn = (V, "sV") if (sl == 0) else (A_, "sA")
                    b.wait(eng, lch, tok)
                    b.wait(eng, sch)
                    if eng is V:
                        ins = V.tensor_copy(pcb[sl][:, 0:wsub], stg[sl][:, 0:wsub])
                    else:
                        ins = A_.copy(pcb[sl][:, 0:wsub], stg[sl][:, 0:wsub])
                    pstate_pro["cast_done"][sl] = (semn, b.inc(semn, ins))
                    b.wait(P, semn, pstate_pro["cast_done"][sl][1])
                    st_tok = dma(sch, P, wsc_view(blk["off"] + j, wsub), pcb[sl][:, 0:wsub])
                    j += wsub
                blk["ready"] = ("sWS%d" % ((pstate_pro["i"] - 1) % 2), st_tok)

        def pair_blocks(p0):
            out = []
            for e in (p0, p0 + 1):
                for a1, a3 in zip(WSLICES["we1t_%d" % e], WSLICES["we3t_%d" % e]):
                    out += [a1, a3]
            for mb in range(8):
                out += [WSLICES["we2t_%d" % p0][mb], WSLICES["we2t_%d" % (p0 + 1)][mb]]
            return out

        pump_sched = {}

        def pump(stage):
            idxs = pump_sched.pop(stage, None)
            if idxs:
                emit_prologue(idxs)


        lead = [WSLICES["gwt"][0]]
        for a1, a3 in zip(WSLICES["ws1t"], WSLICES["ws3t"]):
            lead += [a1, a3]
        lead += WSLICES["ws2t"]
        lead += pair_blocks(0)
        emit_prologue(lead)
        win_ = WSLICES["w_in"]
        pump_sched["pre0:pr2"] = pair_blocks(2)
        pump_sched["pre0:pr4"] = pair_blocks(4)
        pump_sched["pre0:pr6"] = pair_blocks(6) + list(win_[8:25])
        w13 = []
        for a1, a3 in zip(WSLICES["w1"], WSLICES["w3"]):
            w13 += [a1, a3]
        pump_sched["attn:proj"] = list(win_[0:8]) + list(WSLICES["w_out"]) + w13
        pump_sched["attn:ssd"] = list(WSLICES["w2"])
        gw_blk = BLOCKS[WSLICES["gwt"][0]]
        b.wait(S, gw_blk["ready"][0], gw_blk["ready"][1])
        gw_tok = dma("sPR", S, gwt_s[:, 0:128], wsc_view(gw_blk["off"], 128))

        wstate = dict(slot_fence=[0, 0, 0], tok={}, owner=[None, None, None])

        def queue_block(bi):
            if bi in wstate["tok"]:
                return True
            sl = None
            for s2 in range(len(wb)):
                if wstate["owner"][s2] is None:
                    sl = s2
                    break
            if sl is None:
                return False
            blk = BLOCKS[bi]
            b.wait(S, blk["ready"][0], blk["ready"][1])
            b.wait(S, "sPE", wstate["slot_fence"][sl])
            tok = dma("sW%d" % sl, S, wb[sl][:, 0:blk["cols"]],
                      wsc_view(blk["off"], blk["cols"]))
            wstate["tok"][bi] = (sl, tok)
            wstate["owner"][sl] = bi
            return True

        def block_tile(bi, ki, j0, jw):
            sl, tok = wstate["tok"][bi]
            blk = BLOCKS[bi]
            return wb[sl][:, ki * blk["mw"] + j0: ki * blk["mw"] + j0 + jw], ("sW%d" % sl, tok)

        def retire_block(bi, pe_val):
            sl, tok = wstate["tok"].pop(bi)
            wstate["owner"][sl] = None
            wstate["slot_fence"][sl] = pe_val

        pfence = [("sV", 0)] * 4
        pstate = dict(rr=0)

        def acquire_psum():
            s = pstate["rr"]; pstate["rr"] = (s + 1) % 4
            sem, v = pfence[s]
            b.wait(T, sem, v)
            return s, pquad[:, s * 512:(s + 1) * 512]

        def gemm(entries):
            for gi, ent in enumerate(entries):
                g = ent["g"]
                assert len(g["blocks"]) <= len(wb)
                for bi2 in g["blocks"]:
                    ok = queue_block(bi2)
                    assert ok, "current group blocks must fit the ring"
                done = False
                for gj in range(gi + 1, len(entries)):
                    for bi2 in entries[gj]["g"]["blocks"]:
                        if not queue_block(bi2):
                            done = True
                            break
                    if done:
                        break
                b.wait(T, ent["ready"][0], ent["ready"][1])
                last_pe = 0
                for (j0, mw, tg) in g["msubs"]:
                    s, pvw = acquire_psum()
                    nk = len(g["ktiles"])
                    ins = None
                    for t_i, (bpos, ki, kg) in enumerate(g["ktiles"]):
                        bi2 = g["blocks"][bpos]
                        lhsT, tok = block_tile(bi2, ki, j0, mw)
                        b.wait(T, tok[0], tok[1])
                        ins = T.matmul(pvw[0:mw, :], lhsT, ent["rhs"](kg),
                                       start=(t_i == 0), stop=(t_i == nk - 1))
                    pe_val = b.inc("sPE", ins)
                    last_pe = pe_val
                    sem, v = ent["evac"](tg, pvw[0:mw, :], pe_val)
                    pfence[s] = (sem, v)
                for bi2 in g["blocks"]:
                    retire_block(bi2, last_pe)

        def mg_typeA(widxs, tagf):
            out = []
            for bi2 in widxs:
                blk = BLOCKS[bi2]
                msubs = []
                j0 = 0
                while j0 < blk["mw"]:
                    mw = min(128, blk["mw"] - j0)
                    msubs.append((j0, mw, tagf(blk["m0"] + j0, mw)))
                    j0 += mw
                out.append(dict(blocks=[bi2], msubs=msubs,
                                ktiles=[(0, k, k) for k in range(blk["nk"])]))
            return out

        def bcast_mm(row_ap, psum_view, ready):
            b.wait(T, ready[0], ready[1])
            bp = row_ap.base_partition()
            ones_r = onesf[bp:bp + 1, :]
            ins = None
            for j in range(0, psum_view.shape[1], 512):
                w = min(512, psum_view.shape[1] - j)
                ins = T.matmul(psum_view[:, j:j + w], ones_r,
                               row_ap[:, j:j + w], start=True, stop=True)
            return b.inc("sPE", ins)

        def bounce(slot, row_ap, width, dst, prod):
            ch = "sB%d" % (slot % 4)
            b.wait(P, prod[0], prod[1])
            t1 = dma(ch, P, rowsc[slot:slot + 1, 0:width], row_ap)
            b.wait(P, ch, t1)
            t2 = dma(ch, P, dst, rowsc[slot:slot + 1, 0:width].to_broadcast((128, width)))
            return (ch, t2)

        def rhs_xn(k):
            return xn[:, k * TT:(k + 1) * TT]

        def ln_stats(srcf32, xbf, xsq):
            xr_ = b.x_ready
            b.wait(V, xr_[0], xr_[1])
            b.wait(A_, xr_[0], xr_[1])
            toks = []
            for s4 in range(4):
                sl = slice(s4 * 4 * TT, (s4 + 1) * 4 * TT)
                pv_ = b.inc("sV", V.tensor_copy(xbf[:, sl], srcf32[:, sl]))
                pa_ = b.inc("sA", A_.activation(xsq[:, sl], srcf32[:, sl], AF.Square))
                toks.append((pv_, pa_))
            i2 = None
            for s4 in range(4):
                b.wait(T, "sV", toks[s4][0])
                b.wait(T, "sA", toks[s4][1])
                for k in range(s4 * 4, s4 * 4 + 4):
                    T.matmul(pq8[0:1, 512:1024], onesb[:], xbf[:, k * TT:(k + 1) * TT],
                             start=(k == 0), stop=(k == DC - 1))
                for k in range(s4 * 4, s4 * 4 + 4):
                    i2 = T.matmul(pq8[0:1, 1024:1536], onesb[:], xsq[:, k * TT:(k + 1) * TT],
                                  start=(k == 0), stop=(k == DC - 1))
            pe2 = b.inc("sPE", i2)
            b.wait(A_, "sPE", pe2)
            aser(A_.activation(row(0), pq8[0:1, 512:1024], AF.Copy, scale=1.0 / D))
            aA2 = b.inc("sA", A_.copy(row(2), row(0)))
            b.wait(V, "sA", aA2)
            vser(V.tensor_tensor(row(3), row(0), row(2), op=ALU.mult))
            vV = b.inc("sV", V.scalar_tensor_tensor(
                row(2), pq8[0:1, 1024:1536], 1.0 / D, row(3), op0=ALU.mult, op1=ALU.subtract))
            b.wait(A_, "sV", vV)
            aser(A_.activation(row(3), row(2), AF.Ln, bias=EPS))
            aR = b.inc("sA", A_.activation(row(1), row(3), AF.Exp, scale=-0.5))
            return ("sA", aR), ("sA", aR)

        def ffn_call(name):
            with nc.sbuf_tensor("fx_" + name, [128, DC * TT], BF16) as xbf, \
                 nc.sbuf_tensor("fq_" + name, [128, DC * TT], BF16) as xsq, \
                 nc.sbuf_tensor("fh_" + name, [128, FFC * TT], BF16) as h1s, \
                 nc.sbuf_tensor("fa0_" + name, [128, FFC * TT], BF16) as act0, \
                 nc.sbuf_tensor("fa1_" + name, [128, FFC * TT], BF16) as act1, \
                 nc.sbuf_tensor("fcb_" + name, [128, 2 * TT], BF16) as cbuf, \
                 nc.sbuf_tensor("fg_" + name, [128, 1024], F32) as gt:
                rows_rdy, _ = ln_stats(xA, xbf, xsq)
                b.wait(T, "sV"); b.wait(T, "sA")
                pm = bcast_mm(row(0), pq8[:, 0:512], rows_rdy)
                pr = bcast_mm(row(1), pq8[:, 1536:2048], rows_rdy)
                b.wait(V, "sPE", pr)
                V.tensor_tensor(xbf[:, :].rearrange("p (c t) -> p c t", t=TT),
                                xA[:, :].rearrange("p (c t) -> p c t", t=TT),
                                pq8[:, 0:512].unsqueeze(1).broadcast_to([128, DC, TT]),
                                op=ALU.subtract)
                vXN = b.inc("sV", V.tensor_tensor(
                    xn[:, :].rearrange("p (c t) -> p c t", t=TT),
                    xbf[:, :].rearrange("p (c t) -> p c t", t=TT),
                    pq8[:, 1536:2048].unsqueeze(1).broadcast_to([128, DC, TT]),
                    op=ALU.mult))
                rhs_ready = ("sV", vXN)
                tap(name + "_xn", xn[:, :], [128, DC * TT], BF16)

                b.wait(V, "sCF"); b.wait(V, "sTP")
                V.memset(gt[:, :], 0.0)
                b.wait(T, rhs_ready[0], rhs_ready[1])
                b.wait(T, "sPR", gw_tok)
                b.wait(T, "sA"); b.wait(T, "sV")
                ig = None
                for k in range(DC):
                    ig = T.matmul(pq8[0:8, 1536:2048], gwt_s[:, k * 8:(k + 1) * 8], rhs_xn(k),
                                  start=(k == 0), stop=(k == DC - 1))
                peg = b.inc("sPE", ig)
                b.wait(A_, "sPE", peg)
                aE = b.inc("sA", A_.activation(gt[0:8, 0:TT], pq8[0:8, 1536:2048],
                                               AF.Exp, scale=-1.0))
                b.wait(T, "sA", aE)
                it = None
                for c in range(4):
                    it = T.transpose(pq8[:, 1536 + c * 8:1544 + c * 8],
                                     gt[0:8, c * 128:(c + 1) * 128], identf[0:8, 0:8])
                pet = b.inc("sPE", it)
                b.wait(V, "sPE", pet)
                ga = gt[:, 512:1024]
                et = ga[:, 0:32]
                vser(V.tensor_copy(et, pq8[:, 1536:1568]))
                vser(V.tensor_scalar(ga[:, 32:64], et, 1.0, None, op0=ALU.add))
                vser(V.reciprocal(ga[:, 64:96], ga[:, 32:64]))
                sig = ga[:, 64:96]
                vser(V.tensor_tensor(ga[:, 96:128].rearrange("p (c e) -> p c e", e=8),
                                sig.rearrange("p (c e) -> p c e", e=8),
                                biasbc[:, :].unsqueeze(1).broadcast_to([128, 4, 8]),
                                op=ALU.add))
                r = ga[:, 96:128]
                vser(V.tensor_reduce(ga[:, 128:132], r.rearrange("p (c e) -> p c e", e=8),
                                axis=AX, op=ALU.max))
                vser(V.tensor_tensor(ga[:, 132:164].rearrange("p (c e) -> p c e", e=8),
                                r.rearrange("p (c e) -> p c e", e=8),
                                ga[:, 128:132].unsqueeze(2).broadcast_to([128, 4, 8]),
                                op=ALU.is_equal))
                mask1 = ga[:, 132:164]
                vser(V.scalar_tensor_tensor(ga[:, 164:196], mask1, -1e30, r,
                                       op0=ALU.mult, op1=ALU.add))
                r2 = ga[:, 164:196]
                vser(V.tensor_reduce(ga[:, 196:200], r2.rearrange("p (c e) -> p c e", e=8),
                                axis=AX, op=ALU.max))
                vser(V.tensor_tensor(ga[:, 200:232].rearrange("p (c e) -> p c e", e=8),
                                r2.rearrange("p (c e) -> p c e", e=8),
                                ga[:, 196:200].unsqueeze(2).broadcast_to([128, 4, 8]),
                                op=ALU.is_equal))
                mask2 = ga[:, 200:232]
                vser(V.tensor_tensor(ga[:, 232:264], mask1, mask2, op=ALU.add))
                vser(V.tensor_tensor(ga[:, 264:296], sig, ga[:, 232:264], op=ALU.mult))
                selg = ga[:, 264:296]
                vser(V.tensor_reduce(ga[:, 296:300], selg.rearrange("p (c e) -> p c e", e=8),
                                axis=AX, op=ALU.add))
                vser(V.reciprocal(ga[:, 300:304], ga[:, 296:300]))
                vC = b.inc("sV", V.scalar_tensor_tensor(
                    ga[:, 304:336].rearrange("p (c e) -> p c e", e=8),
                    selg.rearrange("p (c e) -> p c e", e=8), 0.5,
                    ga[:, 300:304].unsqueeze(2).broadcast_to([128, 4, 8]),
                    op0=ALU.mult, op1=ALU.mult))
                ct = ga[:, 304:336]
                tap(name + "_gt", gt[:, :], [128, 1024], F32)
                tap(name + "_ct", ct, [128, 32], F32)
                b.wait(T, "sV", vC)
                itc = None
                for c in range(4):
                    itc = T.transpose(pq8[0:8, 1536 + c * 128:1536 + (c + 1) * 128],
                                      ct[:, c * 8:(c + 1) * 8], identf[:, 0:128])
                pec = b.inc("sPE", itc)
                b.wait(V, "sPE", pec)
                cfmb = gt[0:8, 0:256].bitcast(BF16)
                vCf = b.inc("sV", V.tensor_copy(cfmb, pq8[0:8, 1536:2048]))
                b.wait(P, "sV", vCf)
                tcb = dma("sCF", P, rowscb[:, :], cfmb)

                def h1_evac(tg, pvw, pe_val):
                    (fi,) = tg
                    b.wait(A_, "sPE", pe_val)
                    return "sA", b.inc("sA", A_.activation(
                        h1s[:, fi * TT:(fi + 1) * TT], pvw, AF.Silu))

                def mk_h3_evac(dstbuf, coeff):
                    def f(tg, pvw, pe_val):
                        (fi,) = tg
                        b.wait(V, "sPE", pe_val)
                        b.wait(V, "sA")
                        return "sV", b.inc("sV", V.scalar_tensor_tensor(
                            dstbuf[:, fi * TT:(fi + 1) * TT], pvw, coeff,
                            h1s[:, fi * TT:(fi + 1) * TT], op0=ALU.mult, op1=ALU.mult))
                    return f

                def run_12(w1n, w3n, dstbuf, coeff):
                    g1 = mg_typeA(WSLICES[w1n], lambda m0, mw: (m0 // 128,))
                    g3 = mg_typeA(WSLICES[w3n], lambda m0, mw: (m0 // 128,))
                    ents = []
                    for ga_, gc in zip(g1, g3):
                        ents.append(dict(g=ga_, rhs=rhs_xn, evac=h1_evac, ready=rhs_ready))
                        ents.append(dict(g=gc, rhs=rhs_xn, evac=mk_h3_evac(dstbuf, coeff),
                                         ready=rhs_ready))
                    gemm(ents)

                def out_evac(tg, pvw, pe_val):
                    (mc,) = tg
                    b.wait(V, "sPE", pe_val)
                    return "sV", b.inc("sV", V.tensor_tensor(
                        xA[:, mc * TT:(mc + 1) * TT], xA[:, mc * TT:(mc + 1) * TT],
                        pvw, op=ALU.add))

                def run_2(w2lists, actbufs, ready2):
                    nexp = len(w2lists)
                    ents = []
                    for mbi in range(8):
                        gblocks = [w2lists[x][mbi] for x in range(nexp)]
                        ktiles = []
                        for x in range(nexp):
                            for k in range(FFC):
                                ktiles.append((x, k, (x, k)))
                        m0 = BLOCKS[gblocks[0]]["m0"]
                        g = dict(blocks=gblocks, ktiles=ktiles,
                                 msubs=[(0, 128, (m0 // 128,)), (128, 128, (m0 // 128 + 1,))])

                        def rhs2(kg, _bufs=actbufs):
                            x, k = kg
                            return _bufs[x][:, k * TT:(k + 1) * TT]
                        ents.append(dict(g=g, rhs=rhs2, evac=out_evac, ready=ready2))
                    gemm(ents)

                run_12("ws1t", "ws3t", act0, 0.5)
                tap(name + "_h1s", h1s[:, :], [128, FFC * TT], BF16)
                tap(name + "_act", act0[:, :], [128, FFC * TT], BF16)
                run_2([WSLICES["ws2t"]], [act0], ("sV", b.val("sV")))
                tap(name + "_xsh", xA[:, :], [128, DC * TT], F32)
                for p0 in range(0, NE, 2):
                    pump(name + ":pr%d" % p0)
                    pairbufs = [act0, act1]
                    last_pC = 0
                    for xi, e in enumerate((p0, p0 + 1)):
                        dst = pairbufs[xi]
                        run_12("we1t_%d" % e, "we3t_%d" % e, dst, 1.0)
                        b.wait(P, "sCF", tcb)
                        tb = dma("sCB", P, cbuf[:, xi * TT:(xi + 1) * TT],
                                 rowscb[e:e + 1, :].to_broadcast((128, TT)))
                        b.wait(P, "sCB", tb)
                        b.wait(P, "sV"); b.wait(P, "sPE")
                        last_pC = b.inc("sP", P.tensor_tensor(
                            dst[:, :].rearrange("p (f t) -> p f t", t=TT),
                            dst[:, :].rearrange("p (f t) -> p f t", t=TT),
                            cbuf[:, xi * TT:(xi + 1) * TT].unsqueeze(1)
                                .broadcast_to([128, FFC, TT]), op=ALU.mult))
                    run_2([WSLICES["we2t_%d" % p0], WSLICES["we2t_%d" % (p0 + 1)]],
                          pairbufs, ("sP", last_pC))
                b.x_ready = ("sV", b.val("sV"))
                tap(name + "_x", xA[:, :], [128, DC * TT], F32)

        sst = dict(csf_rd=None, csrow_rd=0)

        def ssd_chunk(c, xbc, dtf, dAf, csf3, csrow, ECSa, LLa, MPa, XTa, BTa):
            t0 = c * CH
            xr = xbc[:, :].rearrange("p (c w) -> p c w", w=XBCW)
            csf = csf3[:, 0:CH]
            dlf = csf3[:, CH:2 * CH]
            clf = csf3[:, 2 * CH:3 * CH]
            if sst["csf_rd"]:
                b.wait(V, sst["csf_rd"][0], sst["csf_rd"][1])
            b.wait(V, "sA")
            vCS = vser(V.tensor_tensor_scan(
                csf, dAf[:, t0:t0 + CH], zeros16[:, :], 0.0,
                op0=ALU.add, op1=ALU.add))
            b.wait(P, "sV", vCS)
            if sst["csrow_rd"]:
                b.wait(P, "sPE", sst["csrow_rd"])
            t0a = dma("sSD1", P, rowsc[5, :].rearrange("(h t) -> h t", t=CH), csf)
            sst["csf_rd"] = ("sSD1", t0a)
            b.wait(P, "sSD1", t0a)
            t1 = dma("sSD1", P, csrow[:, :], rowsc[5:6, 0:NH * CH])
            V.tensor_scalar(dlf, csf, csf[:, 127:128], None, op0=ALU.subtract)
            vDL = b.inc("sV", V.tensor_scalar(clf, csf, 0.0, csf[:, 127:128],
                                              op0=ALU.mult, op1=ALU.add))
            b.wait(T, "sV", vDL)
            T.transpose(pq8[:, 512:528], csf, identf[0:16, 0:16])
            T.transpose(pq8[:, 528:544], dtf[:, t0:t0 + CH], identf[0:16, 0:16])
            T.transpose(pq8[:, 544:560], dlf, identf[0:16, 0:16])
            iT = T.transpose(pq8[:, 560:576], clf, identf[0:16, 0:16])
            peT = b.inc("sPE", iT)
            b.wait(V, "sPE", peT)
            vser(V.tensor_copy(smf[:, 0:16], pq8[:, 512:528]))
            vT = vser(V.tensor_copy(smf[:, 16:32], pq8[:, 528:544]))
            b.wait(A_, "sPE", peT)
            b.wait(A_, "sV")
            A_.activation(smf[:, 48:64], pq8[:, 544:560], AF.Exp, scale=-1.0)
            aE2 = b.inc("sA", A_.activation(smf[:, 64:80], pq8[:, 560:576], AF.Exp))
            for s in range(4):
                sem, v = pfence[s]
                b.wait(T, sem, v)
            b.wait(T, "sSD1", t1)
            peCS = bcast_mm(csrow, pquad[:, :], ("sSD1", t1))
            sst["csrow_rd"] = peCS
            b.wait(V, "sPE", peCS)
            V.scalar_tensor_tensor(
                LLa[:, :].rearrange("p (h t) -> p h t", t=CH),
                smf[:, 0:16].unsqueeze(2).broadcast_to([128, NH, CH]), -1.0,
                pquad[:, :].rearrange("p (h t) -> p h t", t=CH),
                op0=ALU.mult, op1=ALU.add)
            vLL = b.inc("sV", V.tensor_tensor(
                LLa[:, :].rearrange("p (h t) -> p h t", t=CH),
                LLa[:, :].rearrange("p (h t) -> p h t", t=CH),
                maskadd[:, :].unsqueeze(1).broadcast_to([128, NH, CH]), op=ALU.add))
            b.wait(A_, "sV", vLL)
            aLE = b.inc("sA", A_.activation(LLa[:, :], LLa[:, :], AF.Exp))
            aEC = b.inc("sA", A_.activation(ECSa[:, :], pquad[:, :], AF.Exp))
            b.wait(T, "sV", vLL)
            b.wait(T, "sA")
            ig = None
            for h in range(NH):
                g = h // 2
                ig = T.matmul(pquad[:, h * CH:(h + 1) * CH],
                              xbc[:, (8 + g) * XBCW + 3 + t0:(8 + g) * XBCW + 3 + t0 + CH],
                              xbc[:, (16 + h) * XBCW + 3 + t0:(16 + h) * XBCW + 3 + t0 + CH],
                              start=True, stop=True)
            peG = b.inc("sPE", ig)
            ix = None
            for g in range(NG):
                ix = T.transpose(ptp[:, g * CH:(g + 1) * CH],
                                 xbc[:, g * XBCW + 3 + t0:g * XBCW + 3 + t0 + CH],
                                 identb[:, :])
            peX = b.inc("sPE", ix)
            b.wait(V, "sPE", peG)
            b.wait(V, "sA", aLE)
            vMP = b.inc("sV", V.tensor_tensor(MPa[:, :], pquad[:, :], LLa[:, :], op=ALU.mult))
            b.wait(V, "sPE", peX)
            vXT = b.inc("sV", V.tensor_tensor(
                XTa[:, :].rearrange("p (g r t) -> p g r t", r=2, t=CH),
                ptp[:, :].rearrange("p (g t) -> p g t", t=CH).unsqueeze(2)
                    .broadcast_to([128, NG, 2, CH]),
                smf[:, 16:32].rearrange("p (g r) -> p g r", r=2).unsqueeze(3)
                    .broadcast_to([128, NG, 2, CH]), op=ALU.mult))
            b.wait(T, "sV", vXT)
            ib = None
            for g in range(NG):
                ib = T.transpose(ptp[:, g * CH:(g + 1) * CH],
                                 xbc[:, (8 + g) * XBCW + 3 + t0:(8 + g) * XBCW + 3 + t0 + CH],
                                 identb[:, :])
            peB = b.inc("sPE", ib)
            b.wait(V, "sPE", peB)
            b.wait(V, "sA", aE2)
            vBT = b.inc("sV", V.tensor_tensor(
                BTa[:, :].rearrange("p (g r t) -> p g r t", r=2, t=CH),
                ptp[:, :].rearrange("p (g t) -> p g t", t=CH).unsqueeze(2)
                    .broadcast_to([128, NG, 2, CH]),
                smf[:, 48:64].rearrange("p (g r) -> p g r", r=2).unsqueeze(3)
                    .broadcast_to([128, NG, 2, CH]), op=ALU.mult))
            b.wait(P, "sA", aEC)
            b.wait(P, "sPE", peG)
            b.wait(P, "sV")
            pCT = b.inc("sP", P.tensor_tensor(
                ECSa[:, :].rearrange("p (h t) -> p h t", t=CH),
                ECSa[:, :].rearrange("p (h t) -> p h t", t=CH),
                xr[:, 16:32, 3 + t0:3 + t0 + CH], op=ALU.mult))
            pSB = b.inc("sP", P.tensor_copy(Stb[:, :], St[:, :]))
            b.wait(T, "sV", vMP)
            b.wait(T, "sV", vBT)
            b.wait(T, "sP", pSB)
            iy = None
            for h in range(NH):
                T.matmul(pquad[:, h * CH:(h + 1) * CH], XTa[:, h * CH:(h + 1) * CH],
                         MPa[:, h * CH:(h + 1) * CH], start=True, stop=False)
                iy = T.matmul(pquad[:, h * CH:(h + 1) * CH], Stb[:, h * CH:(h + 1) * CH],
                              ECSa[:, h * CH:(h + 1) * CH], start=False, stop=True)
            peY = b.inc("sPE", iy)
            b.wait(V, "sPE", peY)
            V.tensor_tensor(MPa[:, :].rearrange("p (g r t) -> p g r t", r=2, t=CH),
                            xr[:, 0:8, 3 + t0:3 + t0 + CH].unsqueeze(2)
                            .broadcast_to([128, NG, 2, CH]),
                            Dbc[:, :].rearrange("p (g r) -> p g r", r=2).unsqueeze(3)
                            .broadcast_to([128, NG, 2, CH]), op=ALU.mult)
            vY = b.inc("sV", V.tensor_tensor(
                yb[:, :].rearrange("p (hc t) -> p hc t", t=TT)[:, :, t0:t0 + CH],
                MPa[:, :].rearrange("p (h t) -> p h t", t=CH),
                pquad[:, :].rearrange("p (h t) -> p h t", t=CH), op=ALU.add))
            for s in range(4):
                pfence[s] = ("sV", vY)
            for half in range(2):
                b.wait(T, "sV")
                b.wait(T, "sA", aEC)
                is_ = None
                for hh in range(8):
                    h = half * 8 + hh
                    is_ = T.matmul(pq8[:, 1024 + hh * CH:1024 + (hh + 1) * CH],
                                   BTa[:, h * CH:(h + 1) * CH], XTa[:, h * CH:(h + 1) * CH],
                                   start=True, stop=True)
                peS = b.inc("sPE", is_)
                b.wait(V, "sPE", peS)
                b.wait(V, "sP", pSB)
                V.tensor_tensor(
                    St[:, half * 1024:(half + 1) * 1024].rearrange("p (g t) -> p g t", t=CH),
                    St[:, half * 1024:(half + 1) * 1024].rearrange("p (g t) -> p g t", t=CH),
                    smf[:, 64 + half * 8:72 + half * 8].unsqueeze(2)
                        .broadcast_to([128, 8, CH]), op=ALU.mult)
                b.inc("sV", V.tensor_tensor(
                    St[:, half * 1024:(half + 1) * 1024],
                    St[:, half * 1024:(half + 1) * 1024],
                    pq8[:, 1024:2048], op=ALU.add))

        def attn_call(tile_in_seq):
            with nc.sbuf_tensor(uname("a_res"), [128, DC * TT], BF16) as resb, \
                 nc.sbuf_tensor(uname("a_u0"), [1, 1], F32) as _unused0:
                with nc.sbuf_tensor(uname("a_xbf"), [128, DC * TT], BF16) as xbf, \
                     nc.sbuf_tensor(uname("a_xsq"), [128, DC * TT], BF16) as xsq:
                    rows_rdy, _ = ln_stats(xA, xbf, xsq)
                    b.wait(A_, rows_rdy[0], rows_rdy[1])
                    aL = aser(A_.activation(row(2), row(2), AF.Ln))
                    b.wait(V, "sA", aL)
                    vq = b.inc("sV", V.tensor_tensor(row(2), row(2), row(3), op=ALU.subtract))
                    b.wait(A_, "sV", vq)
                    aser(A_.activation(row(2), row(2), AF.Exp))
                    aser(A_.activation(row(2), row(2), AF.Ln, bias=EPS))
                    aq = b.inc("sA", A_.activation(row(2), row(2), AF.Exp, scale=-0.5))
                    b.wait(T, "sV"); b.wait(T, "sA")
                    pm = bcast_mm(row(0), pq8[:, 0:512], rows_rdy)
                    pr = bcast_mm(row(1), pq8[:, 1536:2048], rows_rdy)
                    p2 = bcast_mm(row(2), pq8[:, 512:1024], ("sA", aq))
                    sst["csrow_rd"] = p2
                    b.wait(V, "sPE", pr)
                    V.tensor_tensor(xbf[:, :].rearrange("p (c t) -> p c t", t=TT),
                                    xA[:, :].rearrange("p (c t) -> p c t", t=TT),
                                    pq8[:, 0:512].unsqueeze(1).broadcast_to([128, DC, TT]),
                                    op=ALU.subtract)
                    vRES = b.inc("sV", V.tensor_tensor(
                        resb[:, :].rearrange("p (c t) -> p c t", t=TT),
                        xbf[:, :].rearrange("p (c t) -> p c t", t=TT),
                        pq8[:, 1536:2048].unsqueeze(1).broadcast_to([128, DC, TT]),
                        op=ALU.mult))
                    b.wait(V, "sPE", p2)
                    V.tensor_tensor(xbf[:, :].rearrange("p (c t) -> p c t", t=TT),
                                    resb[:, :].rearrange("p (c t) -> p c t", t=TT),
                                    pq8[:, 512:1024].unsqueeze(1).broadcast_to([128, DC, TT]),
                                    op=ALU.mult)
                    vXN = b.inc("sV", V.tensor_tensor(
                        xn[:, :].rearrange("p (c t) -> p c t", t=TT),
                        xbf[:, :].rearrange("p (c t) -> p c t", t=TT),
                        lnw["ln1wc"][:, :].unsqueeze(2).broadcast_to([128, DC, TT]),
                        op=ALU.mult))
                rhs_ready = ("sV", vXN)
                tap("attn_xn", xn[:, :], [128, DC * TT], BF16)
                tap("attn_res", resb[:, :], [128, DC * TT], BF16)

                mixer_es = ExitStack()
                xbc = mixer_es.enter_context(nc.sbuf_tensor(uname("a_xbc"), [128, 32 * XBCW], BF16))
                dtf = rowar[0:16, 0:TT]
                dAf = rowar[0:16, TT:2 * TT]
                u16 = rowar[0:16, 2 * TT:3 * TT]
                cvb = mixer_es.enter_context(nc.sbuf_tensor(uname("a_cv"), [128, 3 * TT], BF16))
                xr = xbc[:, :].rearrange("p (c w) -> p c w", w=XBCW)
                silu_tok = {}

                def conv_ch(ch):
                    base = ch * XBCW
                    sl = ch % 3
                    tmp = cvb[:, sl * TT:(sl + 1) * TT]
                    if sl in silu_tok:
                        b.wait(V, "sA", silu_tok[sl])
                    if tile_in_seq == 0:
                        V.memset(xbc[:, base:base + 3], 0.0)
                    else:
                        V.tensor_copy(xbc[:, base:base + 3], tailb[:, ch * 3:ch * 3 + 3])
                    ins = V.scalar_tensor_tensor(
                        tmp, xbc[:, base + 0:base + TT],
                        cwt[:, ch * 4:ch * 4 + 1],
                        cbv[:, ch:ch + 1].broadcast_to([128, TT]),
                        op0=ALU.mult, op1=ALU.add)
                    for k in (1, 2, 3):
                        ins = V.scalar_tensor_tensor(
                            tmp, xbc[:, base + k:base + k + TT],
                            cwt[:, ch * 4 + k:ch * 4 + k + 1], tmp,
                            op0=ALU.mult, op1=ALU.add)
                    tok = b.inc("sV", V.tensor_copy(tailb[:, ch * 3:ch * 3 + 3],
                                                    xbc[:, base + TT:base + TT + 3]))
                    b.wait(A_, "sV", tok)
                    silu_tok[sl] = b.inc("sA", A_.activation(
                        xbc[:, base + 3:base + 3 + TT], tmp, AF.Silu))

                def proj_evac(tg, pvw, pe_val):
                    kind, idx = tg
                    if kind == "xbc":
                        b.wait(A_, "sPE", pe_val)
                        v = b.inc("sA", A_.copy(xbc[:, idx * XBCW + 3: idx * XBCW + 3 + TT], pvw))
                        b.wait(V, "sA", v)
                        conv_ch(idx)
                        return "sA", v
                    else:
                        b.wait(A_, "sPE", pe_val)
                        aser(A_.activation(u16[:, :], pvw, AF.Abs, bias=dtb[:, 0:1]))
                        aser(A_.activation(u16[:, :], u16[:, :], AF.Exp, scale=-1.0))
                        aser(A_.activation(u16[:, :], u16[:, :], AF.Ln, bias=1.0))
                        aI = b.inc("sA", A_.activation(dtf[:, :], pvw, AF.Relu, bias=dtb[:, 0:1]))
                        b.wait(V, "sA", aI)
                        vser(V.tensor_tensor(dtf[:, :], dtf[:, :], u16[:, :], op=ALU.add))
                        v = b.inc("sV", V.tensor_scalar(dAf[:, :], dtf[:, :], aneg[:, 0:1],
                                                        None, op0=ALU.mult))
                        return "sV", v

                win = WSLICES["w_in"]
                ents = [dict(g=g_, rhs=rhs_xn, evac=proj_evac, ready=rhs_ready)
                        for g_ in mg_typeA(win[8:24],
                                           lambda m0, mw: ("xbc", (m0 - 2048) // 128))]
                ents += [dict(g=g_, rhs=rhs_xn, evac=proj_evac, ready=rhs_ready)
                         for g_ in mg_typeA(win[24:25], lambda m0, mw: ("dt", 0))]
                gemm(ents)
                pump("attn:proj")
                tap("attn_dt", dtf[:, :], [16, TT], F32)

                if tile_in_seq == 0:
                    b.wait(V, "sPE"); b.wait(V, "sP")
                    b.inc("sV", V.memset(St[:, :], 0.0))
                with nc.sbuf_tensor(uname("a_ecs"), [128, NH * CH], BF16) as ECSa, \
                     nc.sbuf_tensor(uname("a_ll"), [128, NH * CH], F32) as LLa, \
                     nc.sbuf_tensor(uname("a_mp"), [128, NH * CH], BF16) as MPa, \
                     nc.sbuf_tensor(uname("a_xt"), [128, NH * CH], BF16) as XTa, \
                     nc.sbuf_tensor(uname("a_bt"), [128, NH * CH], BF16) as BTa, \
                     nc.sbuf_tensor(uname("a_u0d"), [1, 1], F32) as _u0d:
                    csf3 = rowar[0:16, 3 * TT:3 * TT + 3 * CH]
                    csrow = rowar[32:33, 0:NH * CH]
                    for c in range(NCH):
                        ssd_chunk(c, xbc, dtf, dAf, csf3, csrow, ECSa, LLa, MPa, XTa, BTa)
                pump("attn:ssd")
                tap("attn_y", yb[:, :], [128, DC * TT], BF16)
                mixer_es.close()

                gate_es = ExitStack()
                ynb = gate_es.enter_context(nc.sbuf_tensor(uname("a_yn"), [128, DC * TT], BF16))
                res2 = gate_es.enter_context(nc.sbuf_tensor(uname("a_res2"), [128, DC * TT], F32))
                ztmp = gate_es.enter_context(nc.sbuf_tensor(uname("a_zt"), [128, 2 * TT], BF16))
                if True:
                    zslot = [0]

                    def z_evac(tg, pvw, pe_val):
                        (mc,) = tg
                        sl = zslot[0]; zslot[0] = 1 - sl
                        b.wait(A_, "sPE", pe_val)
                        b.wait(A_, "sV")
                        aZ = b.inc("sA", A_.activation(
                            ztmp[:, sl * TT:(sl + 1) * TT], pvw, AF.Silu))
                        b.wait(V, "sA", aZ)
                        vZ = b.inc("sV", V.tensor_tensor(
                            yb[:, mc * TT:(mc + 1) * TT], yb[:, mc * TT:(mc + 1) * TT],
                            ztmp[:, sl * TT:(sl + 1) * TT], op=ALU.mult))
                        return "sV", vZ
                    gemm([dict(g=g_, rhs=rhs_xn, evac=z_evac, ready=rhs_ready)
                          for g_ in mg_typeA(win[0:8], lambda m0, mw: (m0 // 128,))])
                    vyg = b.val("sV")
                    b.wait(P, "sV", vyg)
                    pq = b.inc("sP", P.tensor_tensor(ynb[:, :], yb[:, :], yb[:, :],
                                                     op=ALU.mult))
                    b.wait(T, "sP", pq)
                    b.wait(T, "sA")
                    i2 = None
                    for k in range(DC):
                        i2 = T.matmul(pq8[0:1, 512:1024], onesb[:],
                                      ynb[:, k * TT:(k + 1) * TT],
                                      start=(k == 0), stop=(k == DC - 1))
                    pe2 = b.inc("sPE", i2)
                    b.wait(A_, "sPE", pe2)
                    aser(A_.activation(row(3), pq8[0:1, 512:1024], AF.Ln, bias=EPS, scale=1.0 / D))
                    aR = b.inc("sA", A_.activation(row(1), row(3), AF.Exp, scale=-0.5))
                    b.wait(T, "sV"); b.wait(T, "sA")
                    pr2 = bcast_mm(row(1), pq8[:, 1536:2048], ("sA", aR))
                    b.wait(V, "sPE", pr2)
                    V.tensor_tensor(ynb[:, :].rearrange("p (c t) -> p c t", t=TT),
                                    yb[:, :].rearrange("p (c t) -> p c t", t=TT),
                                    pq8[:, 1536:2048].unsqueeze(1).broadcast_to([128, DC, TT]),
                                    op=ALU.mult)
                    vYN = b.inc("sV", V.tensor_tensor(
                        ynb[:, :].rearrange("p (c t) -> p c t", t=TT),
                        ynb[:, :].rearrange("p (c t) -> p c t", t=TT),
                        lnw["normwc"][:, :].unsqueeze(2).broadcast_to([128, DC, TT]),
                        op=ALU.mult))
                    tap("attn_yn", ynb[:, :], [128, DC * TT], BF16)

                    def wo_evac(tg, pvw, pe_val):
                        (mc,) = tg
                        b.wait(V, "sPE", pe_val)
                        return "sV", b.inc("sV", V.tensor_tensor(
                            res2[:, mc * TT:(mc + 1) * TT], resb[:, mc * TT:(mc + 1) * TT],
                            pvw, op=ALU.add))

                    def rhs_yn(k):
                        return ynb[:, k * TT:(k + 1) * TT]
                    gemm([dict(g=g_, rhs=rhs_yn, evac=wo_evac, ready=("sV", vYN))
                          for g_ in mg_typeA(WSLICES["w_out"], lambda m0, mw: (m0 // 128,))])
                    tap("attn_res2", res2[:, :], [128, DC * TT], F32)
                    b.wait(P, "sV"); b.wait(P, "sPE")
                    pX = b.inc("sP", P.tensor_tensor(xA[:, :], xA[:, :], res2[:, :],
                                                     op=ALU.add))
                    b.wait(A_, "sV"); b.wait(A_, "sP", pq)
                    aSq = b.inc("sA", A_.activation(yb[:, :], res2[:, :], AF.Square))
                    b.wait(T, "sA", aSq)
                    i3 = None
                    for k in range(DC):
                        i3 = T.matmul(pq8[0:1, 1024:1536], onesb[:],
                                      yb[:, k * TT:(k + 1) * TT],
                                      start=(k == 0), stop=(k == DC - 1))
                    pe3 = b.inc("sPE", i3)
                    b.wait(A_, "sPE", pe3)
                    aser(A_.activation(row(3), pq8[0:1, 1024:1536], AF.Ln, bias=EPS, scale=1.0 / D))
                    aR2 = b.inc("sA", A_.activation(row(1), row(3), AF.Exp, scale=-0.5))
                    b.wait(T, "sV")
                    pr3 = bcast_mm(row(1), pq8[:, 1536:2048], ("sA", aR2))
                    b.wait(V, "sPE", pr3)
                    V.tensor_tensor(yb[:, :].rearrange("p (c t) -> p c t", t=TT),
                                    res2[:, :].rearrange("p (c t) -> p c t", t=TT),
                                    pq8[:, 1536:2048].unsqueeze(1).broadcast_to([128, DC, TT]),
                                    op=ALU.mult)
                    vXN2 = b.inc("sV", V.tensor_tensor(
                        xn[:, :].rearrange("p (c t) -> p c t", t=TT),
                        yb[:, :].rearrange("p (c t) -> p c t", t=TT),
                        lnw["ln2wc"][:, :].unsqueeze(2).broadcast_to([128, DC, TT]),
                        op=ALU.mult))
                    rhs2_ready = ("sV", vXN2)
                    tap("attn_xn2", xn[:, :], [128, DC * TT], BF16)
                gate_es.close()

                with nc.sbuf_tensor(uname("a_act"), [128, 64 * TT], BF16) as actI:
                    def ih1_evac(tg, pvw, pe_val):
                        (fi,) = tg
                        b.wait(A_, "sPE", pe_val)
                        return "sA", b.inc("sA", A_.activation(
                            actI[:, fi * TT:(fi + 1) * TT], pvw, AF.Silu))

                    def ih3_evac(tg, pvw, pe_val):
                        (fi,) = tg
                        b.wait(V, "sPE", pe_val)
                        b.wait(V, "sA")
                        return "sV", b.inc("sV", V.tensor_tensor(
                            actI[:, fi * TT:(fi + 1) * TT], pvw,
                            actI[:, fi * TT:(fi + 1) * TT], op=ALU.mult))

                    g1 = mg_typeA(WSLICES["w1"], lambda m0, mw: (m0 // 128,))
                    g3 = mg_typeA(WSLICES["w3"], lambda m0, mw: (m0 // 128,))
                    ents = []
                    for ga_, gc in zip(g1, g3):
                        ents.append(dict(g=ga_, rhs=rhs_xn, evac=ih1_evac, ready=rhs2_ready))
                        ents.append(dict(g=gc, rhs=rhs_xn, evac=ih3_evac, ready=rhs2_ready))
                    gemm(ents)

                    def iout_evac(tg, pvw, pe_val):
                        (mc,) = tg
                        b.wait(V, "sPE", pe_val)
                        b.wait(V, "sP")
                        return "sV", b.inc("sV", V.tensor_tensor(
                            xA[:, mc * TT:(mc + 1) * TT], xA[:, mc * TT:(mc + 1) * TT],
                            pvw, op=ALU.add))

                    def rhs_act(kg):
                        return actI[:, kg * TT:(kg + 1) * TT]
                    w2l = WSLICES["w2"]
                    ents2 = []
                    for mc in range(16):
                        bA, bB = w2l[2 * mc], w2l[2 * mc + 1]
                        ktiles = ([(0, k, k) for k in range(32)] +
                                  [(1, k, 32 + k) for k in range(32)])
                        ents2.append(dict(g=dict(blocks=[bA, bB], msubs=[(0, 128, (mc,))],
                                                 ktiles=ktiles),
                                          rhs=rhs_act, evac=iout_evac,
                                          ready=("sV", b.val("sV"))))
                    gemm(ents2)
                    b.x_ready = ("sV", b.val("sV"))
                tap("attn_out", xA[:, :], [128, DC * TT], F32)

        out_fence = 0
        for ti in range(N_TILES):
            s_i = ti // TPS
            j_i = ti % TPS
            tok0 = j_i * TT
            b.wait(P, "sXO", out_fence)
            for sn in ("sV", "sA", "sPE", "sP"):
                b.wait(P, sn)
            tin = dma("sXI", P,
                      xA[:, :].rearrange("p (c t) -> p c t", t=TT),
                      dr["xin"][s_i, :, tok0:tok0 + TT].rearrange("(c p) t -> p c t", p=128))
            b.x_ready = ("sXI", tin)
            for e_ in (V, A_, P, T):
                b.wait(e_, "sXI", tin)
            tap("t%d_xin" % ti, xA[:, :], [128, DC * TT], F32)
            ffn_call("pre%d" % ti)
            tap("t%d_x1" % ti, xA[:, :], [128, DC * TT], F32)
            attn_call(j_i)
            tap("t%d_x2" % ti, xA[:, :], [128, DC * TT], F32)
            ffn_call("post%d" % ti)
            for sn in ("sV", "sA", "sP", "sPE"):
                b.wait(P, sn)
            out_fence = dma("sXO", P,
                            dr["xout"][s_i, :, tok0:tok0 + TT].rearrange("(c p) t -> p c t", p=128),
                            xA[:, :].rearrange("p (c t) -> p c t", t=TT))
        b.wait(P, "sXO")
        b.wait(P, "sTP")
    es.close()
    return nc, b


def _host_prep(inputs):
    g = {}
    f = np.float32
    g["w_in"] = np.ascontiguousarray(np.asarray(inputs["L0_attn_W_in"], dtype=f))
    g["w_out"] = np.ascontiguousarray(np.asarray(inputs["L0_attn_W_out"], dtype=f))
    g["w1"] = np.ascontiguousarray(np.asarray(inputs["L0_attn_w1"], dtype=f))
    g["w3"] = np.ascontiguousarray(np.asarray(inputs["L0_attn_w3"], dtype=f))
    g["w2"] = np.ascontiguousarray(np.asarray(inputs["L0_attn_w2"], dtype=f))
    g["ws1t"] = np.ascontiguousarray(np.asarray(inputs["L0_ffn_ws1"], dtype=f).T)
    g["ws3t"] = np.ascontiguousarray(np.asarray(inputs["L0_ffn_ws3"], dtype=f).T)
    g["ws2t"] = np.ascontiguousarray(np.asarray(inputs["L0_ffn_ws2"], dtype=f).T)
    g["gwt"] = np.ascontiguousarray(np.asarray(inputs["L0_ffn_gate_w"], dtype=f).T)
    g["we1t"] = np.ascontiguousarray(np.asarray(inputs["L0_ffn_we1"], dtype=f).transpose(0, 2, 1))
    g["we3t"] = np.ascontiguousarray(np.asarray(inputs["L0_ffn_we3"], dtype=f).transpose(0, 2, 1))
    g["we2t"] = np.ascontiguousarray(np.asarray(inputs["L0_ffn_we2"], dtype=f).transpose(0, 2, 1))
    g["ln1wc"] = np.ascontiguousarray(np.asarray(inputs["L0_attn_ln1_w"], dtype=f).reshape(16, 128).T)
    g["normwc"] = np.ascontiguousarray(np.asarray(inputs["L0_attn_norm_w"], dtype=f).reshape(16, 128).T)
    g["ln2wc"] = np.ascontiguousarray(np.asarray(inputs["L0_attn_ln2_w"], dtype=f).reshape(16, 128).T)
    cw = np.asarray(inputs["L0_attn_conv_w"], dtype=f)
    g["cwt"] = np.ascontiguousarray(cw.T.reshape(32, 128, 4).transpose(1, 0, 2).reshape(128, 128))
    g["cb"] = np.ascontiguousarray(np.asarray(inputs["L0_attn_conv_b"], dtype=f).reshape(32, 128).T)
    g["dtb"] = np.ascontiguousarray(np.asarray(inputs["L0_attn_dt_bias"], dtype=f).reshape(16, 1))
    g["alog"] = np.ascontiguousarray(np.asarray(inputs["L0_attn_A_log"], dtype=f).reshape(16, 1))
    g["drow"] = np.ascontiguousarray(np.asarray(inputs["L0_attn_D"], dtype=f).reshape(1, 16))
    g["biasrow"] = np.ascontiguousarray(np.asarray(inputs["L0_ffn_bias"], dtype=f).reshape(1, 8))
    g["identf"] = np.eye(128, dtype=f)
    g["maskadd"] = np.where(np.arange(128)[:, None] <= np.arange(128)[None, :],
                            f(0), f(-1e30)).astype(f)
    return g


def kernel(**inputs):
    x = np.asarray(inputs["L0_attn_hidden_states"], dtype=np.float32)
    g = _host_prep(inputs)
    nc, b = build_nc()
    in_maps = []
    for c in range(8):
        m = dict(g)
        m["xin"] = np.ascontiguousarray(x[2 * c:2 * c + 2].transpose(0, 2, 1))
        in_maps.append(m)
    res = run_bass_kernel_spmd(nc, in_maps, core_ids=list(range(8)))
    LAST_RESULTS[0] = res
    outs = []
    for c in range(8):
        o = np.asarray(res.results[c]["xout"])
        outs.append(np.ascontiguousarray(o.transpose(0, 2, 1)))
    return np.concatenate(outs, axis=0)


# version 09
# speedup vs baseline: 1.0815x; 1.0815x over previous
import numpy as np
from contextlib import ExitStack
import concourse.bass as bass
import concourse.mybir as mybir
from concourse.bass_utils import run_bass_kernel_spmd

F32 = mybir.dt.float32
BF16 = mybir.dt.bfloat16
AF = mybir.ActivationFunctionType
ALU = mybir.AluOpType
AX = mybir.AxisListType.X

D = 2048
DC = 16
TT = 512
SEQ = 2048
NSEQ = 2
TPS = SEQ // TT
CH = 128
NCH = TT // CH
NH = 16
NG = 8
FF = 1408
FFC = 11
NE = 8
EPS = 1e-5
XBCW = 3 + TT

DEBUG_TAPS = []
N_TILES = NSEQ * TPS
LAST_RESULTS = [None]


def plan_blocks():
    blocks = []
    off = 0
    wslices = {}

    def add(w, K, M, mstep=256, mlist=None):
        nonlocal off
        nkc = K // 128
        idxs = []
        if mlist is None:
            mlist = []
            m0 = 0
            while m0 < M:
                mw = min(mstep, M - m0)
                mlist.append((m0, mw))
                m0 += mw
        for (m0, mw) in mlist:
            blocks.append(dict(w=w, kc0=0, nk=nkc, m0=m0, mw=mw, off=off, cols=nkc * mw))
            idxs.append(len(blocks) - 1)
            off += nkc * mw
        wslices.setdefault(w, []).extend(idxs)
        return idxs

    add("gwt", 2048, 8)
    add("w_in", 2048, 6160, mlist=[(m, 256) for m in range(0, 6144, 256)] + [(6144, 16)])
    add("w_out", 2048, 2048)
    add("w1", 2048, 8192)
    add("w3", 2048, 8192)
    w2l = []
    for m in range(0, 2048, 128):
        for kc0 in (0, 32):
            blocks.append(dict(w="w2", kc0=kc0, nk=32, m0=m, mw=128, off=off, cols=32 * 128))
            w2l.append(len(blocks) - 1)
            off += 32 * 128
    wslices["w2"] = w2l
    add("ws1t", 2048, 1408)
    add("ws3t", 2048, 1408)
    add("ws2t", 1408, 2048)
    for e in range(NE):
        add("we1t_%d" % e, 2048, 1408)
        add("we3t_%d" % e, 2048, 1408)
        add("we2t_%d" % e, 1408, 2048)
    return blocks, wslices, off

BLOCKS, WSLICES, SCRATCH_COLS = plan_blocks()
_half = SCRATCH_COLS // 2
WSC_SPLIT = min(blk["off"] for blk in BLOCKS if blk["off"] >= _half)


class Builder:
    def __init__(self, nc, es):
        self.nc = nc
        self.es = es
        self.sems = {}
        self.waited = {}
        self.x_ready = ("sPE", 0)
        self.T = nc.tensor; self.V = nc.vector; self.A = nc.scalar
        self.P = nc.gpsimd; self.S = nc.sync

    def mksem(self, *names):
        for n in names:
            self.sems[n] = [self.es.enter_context(self.nc.semaphore(n)), 0]

    def inc(self, name, instr, k=1):
        s = self.sems[name]
        instr.then_inc(s[0], k)
        s[1] += k
        return s[1]

    def val(self, name):
        return self.sems[name][1]

    def wait(self, eng, name, v=None, force=False):
        if v is None:
            v = self.sems[name][1]
        key = (id(eng), name)
        if force or self.waited.get(key, -1) < v:
            eng.wait_ge(self.sems[name][0], v)
            if self.waited.get(key, -1) < v:
                self.waited[key] = v

    def hz(self, eng):
        name = "sV" if eng is self.V else ("sA" if eng is self.A else "sP")
        key = (id(eng), name)
        self.wait(eng, name, self.waited.get(key, 0), force=True)


def build_nc():
    nc = bass.Bass("TRN2", target_bir_lowering=False)
    es = ExitStack()
    b = Builder(nc, es)
    b.mksem("sPE", "sV", "sA", "sP")
    b.mksem("sW0", "sW1", "sW2", "sW3", "sWS0", "sWS1", "sXI", "sXO", "sPR",
            "sB0", "sB1", "sB2", "sB3", "sSD1", "sSD2", "sCF", "sCB", "sTP")
    T, V, A_, P, S = b.T, b.V, b.A, b.P, b.S

    def vser(instr):
        v = b.inc("sV", instr)
        b.wait(b.V, "sV", v, force=True)
        return v

    def aser(instr):
        v = b.inc("sA", instr)
        b.wait(b.A, "sA", v, force=True)
        return v

    def dma(chan, eng, out, in_):
        b.wait(eng, chan, b.val(chan), force=False)
        return b.inc(chan, eng.dma_start(out=out, in_=in_), 16)

    dr = {}
    dr["xin"] = nc.dram_tensor("xin", [NSEQ, D, SEQ], F32, kind="ExternalInput").ap()
    dr["xout"] = nc.dram_tensor("xout", [NSEQ, D, SEQ], F32, kind="ExternalOutput").ap()
    for n, s in {
        "w_in": [2048, 6160], "w_out": [2048, 2048], "w1": [2048, 8192],
        "w3": [2048, 8192], "w2": [8192, 2048], "ws1t": [2048, 1408],
        "ws3t": [2048, 1408], "ws2t": [1408, 2048], "gwt": [2048, 8],
        "we1t": [NE, 2048, 1408], "we3t": [NE, 2048, 1408], "we2t": [NE, 1408, 2048],
        "ln1wc": [128, 16], "normwc": [128, 16], "ln2wc": [128, 16],
        "cwt": [128, 128], "cb": [128, 32], "dtb": [16, 1], "alog": [16, 1],
        "drow": [1, 16], "biasrow": [1, 8], "identf": [128, 128], "maskadd": [128, 128],
    }.items():
        dr[n] = nc.dram_tensor(n, s, F32, kind="ExternalInput").ap()
    wsc0 = nc.dram_tensor("wsc0", [128, WSC_SPLIT], BF16, kind="Internal").ap()
    wsc1 = nc.dram_tensor("wsc1", [128, SCRATCH_COLS - WSC_SPLIT], BF16, kind="Internal").ap()

    def wsc_view(off, cols):
        if off < WSC_SPLIT:
            assert off + cols <= WSC_SPLIT
            return wsc0[:, off:off + cols]
        return wsc1[:, off - WSC_SPLIT:off - WSC_SPLIT + cols]
    rowsc = nc.dram_tensor("rowsc", [8, 2048], F32, kind="Internal").ap()
    rowscb = nc.dram_tensor("rowscb", [NE, TT], BF16, kind="Internal").ap()

    _uid = [0]

    def uname(s):
        _uid[0] += 1
        return "%s_u%d" % (s, _uid[0])

    def sb(name, shape, dt):
        return es.enter_context(nc.sbuf_tensor(name, shape, dt))

    xA = sb("xA", [128, DC * TT], F32)
    xn = sb("xn", [128, DC * TT], BF16)
    yb = sb("yb", [128, DC * TT], BF16)
    wb = [sb("wb%d" % i, [128, 4096], BF16) for i in range(3)]
    St = sb("St", [128, NH * 128], F32)
    Stb = sb("Stb", [128, NH * 128], BF16)
    identf = sb("identfs", [128, 128], F32)
    identb = sb("identbs", [128, 128], BF16)
    maskadd = sb("maskadds", [128, 128], F32)
    cwt = sb("cwts", [128, 128], F32)
    cbv = sb("cbs", [128, 32], F32)
    lnw = {n: sb(n + "s", [128, 16], F32) for n in ("ln1wc", "normwc", "ln2wc")}
    dtb = sb("dtbs", [16, 1], F32)
    aneg = sb("anegs", [16, 1], F32)
    Dbc = sb("Dbcs", [128, 16], F32)
    biasbc = sb("biasbcs", [128, 8], F32)
    gwt_s = sb("gwts", [128, 128], BF16)
    onesb = sb("onesb", [128, 1], BF16)
    onesf = sb("onesf", [33, 128], F32)
    zeros16 = sb("zeros16", [16, 128], F32)
    rowar = sb("rowar", [128, 2048], F32)
    smf = sb("smf", [128, 96], F32)
    constv = sb("constv", [128, 2], F32)
    stg = [sb("stg%d" % i, [128, 1024], F32) for i in range(2)]
    pcb = [sb("pc%d" % i, [128, 1024], BF16) for i in range(2)]
    nc.const_aps.register(F32, EPS, constv[:, 0:1])
    nc.const_aps.register(F32, 1.0, constv[:, 1:2])
    tailb = sb("tailb", [128, 96], BF16)
    pquad = es.enter_context(nc.psum_tensor("pquad", [128, 2048], F32))
    pq8 = es.enter_context(nc.psum_tensor("pq8", [128, 2048], F32))
    ptp = pq8[:, 0:512].bitcast(BF16)

    def row(i):
        return rowar[0:1, i * TT:(i + 1) * TT]

    taps = {}

    def tap(name, ap, shape, dt=F32):
        if name in DEBUG_TAPS:
            out = nc.dram_tensor("tap_" + name, list(shape), dt, kind="ExternalOutput").ap()
            for sn in ("sV", "sA", "sP", "sPE"):
                b.wait(P, sn)
            tk = dma("sTP", P, out[:, :], ap)
            b.wait(P, "sTP", tk)
            taps[name] = shape

    with nc.Block() as _blk:
        for (dst, src) in [(identf, dr["identf"]), (maskadd, dr["maskadd"]),
                           (cwt, dr["cwt"]), (cbv, dr["cb"]), (dtb, dr["dtb"]),
                           (lnw["ln1wc"], dr["ln1wc"]), (lnw["normwc"], dr["normwc"]),
                           (lnw["ln2wc"], dr["ln2wc"])]:
            dma("sPR", S, dst[:], src[:, :])
        dma("sPR", S, smf[0:16, 80:81], dr["alog"][:, :])
        dma("sPR", S, Dbc[:], dr["drow"][:, :].to_broadcast((128, 16)))
        pv = dma("sPR", S, biasbc[:], dr["biasrow"][:, :].to_broadcast((128, 8)))
        b.wait(V, "sPR", pv)
        V.memset(onesb[:], 1.0)
        V.memset(onesf[0:1, :], 1.0)
        V.memset(onesf[32:33, :], 1.0)
        V.memset(zeros16[:], 0.0)
        V.memset(constv[:, 0:1], EPS)
        V.memset(constv[:, 1:2], 1.0)
        V.tensor_copy(identb[:], identf[:])
        b.wait(A_, "sPR", pv)
        aA = b.inc("sA", A_.activation(smf[0:16, 81:82], smf[0:16, 80:81], AF.Exp))
        b.wait(V, "sA", aA)
        b.inc("sV", V.tensor_scalar(aneg[:], smf[0:16, 81:82], -1.0, None, op0=ALU.mult))

        pstate_pro = dict(i=0, cast_done=[("sV", 0), ("sV", 0)])

        def emit_prologue(idxs):
            SUB = 1024
            for bi in idxs:
                blk = BLOCKS[bi]
                w = blk["w"]
                if w.startswith("we"):
                    base, e = w.rsplit("_", 1)
                    srcw = dr[base][int(e)]
                else:
                    srcw = dr[w]
                kc0, nk, m0, mw, cols = blk["kc0"], blk["nk"], blk["m0"], blk["mw"], blk["cols"]
                rows = srcw[kc0 * 128:(kc0 + nk) * 128, m0:m0 + mw].rearrange(
                    "(k p) m -> p k m", p=128)
                st_tok = None
                j = 0
                while j < cols:
                    wsub = min(SUB, cols - j)
                    assert wsub % mw == 0 or j % mw == 0
                    k0 = j // mw
                    k1 = (j + wsub) // mw
                    sl = pstate_pro["i"] % 2
                    pstate_pro["i"] += 1
                    lch = "sW%d" % sl
                    sch = "sWS%d" % sl
                    cd = pstate_pro["cast_done"][sl]
                    b.wait(S, cd[0], cd[1])
                    tok = dma(lch, S,
                              stg[sl][:, 0:wsub].rearrange("p (k m) -> p k m", m=mw),
                              rows[:, k0:k1, :])
                    eng, semn = (V, "sV") if (sl == 0) else (A_, "sA")
                    b.wait(eng, lch, tok)
                    b.wait(eng, sch)
                    if eng is V:
                        ins = V.tensor_copy(pcb[sl][:, 0:wsub], stg[sl][:, 0:wsub])
                    else:
                        ins = A_.copy(pcb[sl][:, 0:wsub], stg[sl][:, 0:wsub])
                    pstate_pro["cast_done"][sl] = (semn, b.inc(semn, ins))
                    b.wait(P, semn, pstate_pro["cast_done"][sl][1])
                    st_tok = dma(sch, P, wsc_view(blk["off"] + j, wsub), pcb[sl][:, 0:wsub])
                    j += wsub
                blk["ready"] = ("sWS%d" % ((pstate_pro["i"] - 1) % 2), st_tok)

        def pair_blocks(p0):
            out = []
            for e in (p0, p0 + 1):
                for a1, a3 in zip(WSLICES["we1t_%d" % e], WSLICES["we3t_%d" % e]):
                    out += [a1, a3]
            for mb in range(8):
                out += [WSLICES["we2t_%d" % p0][mb], WSLICES["we2t_%d" % (p0 + 1)][mb]]
            return out

        pump_sched = {}

        def pump(stage):
            idxs = pump_sched.pop(stage, None)
            if idxs:
                emit_prologue(idxs)


        lead = [WSLICES["gwt"][0]]
        for a1, a3 in zip(WSLICES["ws1t"], WSLICES["ws3t"]):
            lead += [a1, a3]
        lead += WSLICES["ws2t"]
        lead += pair_blocks(0)
        emit_prologue(lead)
        win_ = WSLICES["w_in"]
        pump_sched["pre0:pr2"] = pair_blocks(2)
        pump_sched["pre0:pr4"] = pair_blocks(4)
        pump_sched["pre0:pr6"] = pair_blocks(6) + list(win_[8:25])
        w13 = []
        for a1, a3 in zip(WSLICES["w1"], WSLICES["w3"]):
            w13 += [a1, a3]
        pump_sched["attn:proj"] = list(win_[0:8]) + list(WSLICES["w_out"]) + w13
        pump_sched["attn:ssd"] = list(WSLICES["w2"])
        gw_blk = BLOCKS[WSLICES["gwt"][0]]
        b.wait(S, gw_blk["ready"][0], gw_blk["ready"][1])
        gw_tok = dma("sPR", S, gwt_s[:, 0:128], wsc_view(gw_blk["off"], 128))

        wstate = dict(slot_fence=[0, 0, 0], tok={}, owner=[None, None, None])

        def queue_block(bi):
            if bi in wstate["tok"]:
                return True
            sl = None
            for s2 in range(len(wb)):
                if wstate["owner"][s2] is None:
                    sl = s2
                    break
            if sl is None:
                return False
            blk = BLOCKS[bi]
            b.wait(S, blk["ready"][0], blk["ready"][1])
            b.wait(S, "sPE", wstate["slot_fence"][sl])
            tok = dma("sW%d" % sl, S, wb[sl][:, 0:blk["cols"]],
                      wsc_view(blk["off"], blk["cols"]))
            wstate["tok"][bi] = (sl, tok)
            wstate["owner"][sl] = bi
            return True

        def block_tile(bi, ki, j0, jw):
            sl, tok = wstate["tok"][bi]
            blk = BLOCKS[bi]
            return wb[sl][:, ki * blk["mw"] + j0: ki * blk["mw"] + j0 + jw], ("sW%d" % sl, tok)

        def retire_block(bi, pe_val):
            sl, tok = wstate["tok"].pop(bi)
            wstate["owner"][sl] = None
            wstate["slot_fence"][sl] = pe_val

        pfence = [("sV", 0)] * 4
        pstate = dict(rr=0)

        def acquire_psum():
            s = pstate["rr"]; pstate["rr"] = (s + 1) % 4
            sem, v = pfence[s]
            b.wait(T, sem, v)
            return s, pquad[:, s * 512:(s + 1) * 512]

        def gemm(entries):
            for gi, ent in enumerate(entries):
                g = ent["g"]
                assert len(g["blocks"]) <= len(wb)
                for bi2 in g["blocks"]:
                    ok = queue_block(bi2)
                    assert ok, "current group blocks must fit the ring"
                done = False
                for gj in range(gi + 1, len(entries)):
                    for bi2 in entries[gj]["g"]["blocks"]:
                        if not queue_block(bi2):
                            done = True
                            break
                    if done:
                        break
                b.wait(T, ent["ready"][0], ent["ready"][1])
                last_pe = 0
                for (j0, mw, tg) in g["msubs"]:
                    s, pvw = acquire_psum()
                    nk = len(g["ktiles"])
                    ins = None
                    for t_i, (bpos, ki, kg) in enumerate(g["ktiles"]):
                        bi2 = g["blocks"][bpos]
                        lhsT, tok = block_tile(bi2, ki, j0, mw)
                        b.wait(T, tok[0], tok[1])
                        ins = T.matmul(pvw[0:mw, :], lhsT, ent["rhs"](kg),
                                       start=(t_i == 0), stop=(t_i == nk - 1))
                    pe_val = b.inc("sPE", ins)
                    last_pe = pe_val
                    sem, v = ent["evac"](tg, pvw[0:mw, :], pe_val)
                    pfence[s] = (sem, v)
                for bi2 in g["blocks"]:
                    retire_block(bi2, last_pe)

        def mg_typeA(widxs, tagf):
            out = []
            for bi2 in widxs:
                blk = BLOCKS[bi2]
                msubs = []
                j0 = 0
                while j0 < blk["mw"]:
                    mw = min(128, blk["mw"] - j0)
                    msubs.append((j0, mw, tagf(blk["m0"] + j0, mw)))
                    j0 += mw
                out.append(dict(blocks=[bi2], msubs=msubs,
                                ktiles=[(0, k, k) for k in range(blk["nk"])]))
            return out

        def bcast_mm(row_ap, psum_view, ready):
            b.wait(T, ready[0], ready[1])
            bp = row_ap.base_partition()
            ones_r = onesf[bp:bp + 1, :]
            ins = None
            for j in range(0, psum_view.shape[1], 512):
                w = min(512, psum_view.shape[1] - j)
                ins = T.matmul(psum_view[:, j:j + w], ones_r,
                               row_ap[:, j:j + w], start=True, stop=True)
            return b.inc("sPE", ins)

        def bounce(slot, row_ap, width, dst, prod):
            ch = "sB%d" % (slot % 4)
            b.wait(P, prod[0], prod[1])
            t1 = dma(ch, P, rowsc[slot:slot + 1, 0:width], row_ap)
            b.wait(P, ch, t1)
            t2 = dma(ch, P, dst, rowsc[slot:slot + 1, 0:width].to_broadcast((128, width)))
            return (ch, t2)

        def rhs_xn(k):
            return xn[:, k * TT:(k + 1) * TT]

        def ln_stats(srcf32, xbf, xsq):
            xr_ = b.x_ready
            b.wait(V, xr_[0], xr_[1])
            b.wait(A_, xr_[0], xr_[1])
            toks = []
            for s4 in range(4):
                sl = slice(s4 * 4 * TT, (s4 + 1) * 4 * TT)
                pv_ = b.inc("sV", V.tensor_copy(xbf[:, sl], srcf32[:, sl]))
                pa_ = b.inc("sA", A_.activation(xsq[:, sl], srcf32[:, sl], AF.Square))
                toks.append((pv_, pa_))
            i2 = None
            for s4 in range(4):
                b.wait(T, "sV", toks[s4][0])
                b.wait(T, "sA", toks[s4][1])
                for k in range(s4 * 4, s4 * 4 + 4):
                    T.matmul(pq8[0:1, 512:1024], onesb[:], xbf[:, k * TT:(k + 1) * TT],
                             start=(k == 0), stop=(k == DC - 1))
                for k in range(s4 * 4, s4 * 4 + 4):
                    i2 = T.matmul(pq8[0:1, 1024:1536], onesb[:], xsq[:, k * TT:(k + 1) * TT],
                                  start=(k == 0), stop=(k == DC - 1))
            pe2 = b.inc("sPE", i2)
            b.wait(A_, "sPE", pe2)
            aser(A_.activation(row(0), pq8[0:1, 512:1024], AF.Copy, scale=1.0 / D))
            aA2 = b.inc("sA", A_.copy(row(2), row(0)))
            b.wait(V, "sA", aA2)
            vser(V.tensor_tensor(row(3), row(0), row(2), op=ALU.mult))
            vV = b.inc("sV", V.scalar_tensor_tensor(
                row(2), pq8[0:1, 1024:1536], 1.0 / D, row(3), op0=ALU.mult, op1=ALU.subtract))
            b.wait(A_, "sV", vV)
            aser(A_.activation(row(3), row(2), AF.Ln, bias=EPS))
            aR = b.inc("sA", A_.activation(row(1), row(3), AF.Exp, scale=-0.5))
            return ("sA", aR), ("sA", aR)

        def ffn_call(name):
            with nc.sbuf_tensor("fx_" + name, [128, DC * TT], BF16) as xbf, \
                 nc.sbuf_tensor("fq_" + name, [128, DC * TT], BF16) as xsq, \
                 nc.sbuf_tensor("fh_" + name, [128, FFC * TT], BF16) as h1s, \
                 nc.sbuf_tensor("fa0_" + name, [128, FFC * TT], BF16) as act0, \
                 nc.sbuf_tensor("fa1_" + name, [128, FFC * TT], BF16) as act1, \
                 nc.sbuf_tensor("fcb_" + name, [128, 2 * TT], BF16) as cbuf, \
                 nc.sbuf_tensor("fg_" + name, [128, 1024], F32) as gt:
                rows_rdy, _ = ln_stats(xA, xbf, xsq)
                b.wait(T, "sV"); b.wait(T, "sA")
                pm = bcast_mm(row(0), pq8[:, 0:512], rows_rdy)
                pr = bcast_mm(row(1), pq8[:, 1536:2048], rows_rdy)
                b.wait(V, "sPE", pr)
                V.tensor_tensor(xbf[:, :].rearrange("p (c t) -> p c t", t=TT),
                                xA[:, :].rearrange("p (c t) -> p c t", t=TT),
                                pq8[:, 0:512].unsqueeze(1).broadcast_to([128, DC, TT]),
                                op=ALU.subtract)
                vXN = b.inc("sV", V.tensor_tensor(
                    xn[:, :].rearrange("p (c t) -> p c t", t=TT),
                    xbf[:, :].rearrange("p (c t) -> p c t", t=TT),
                    pq8[:, 1536:2048].unsqueeze(1).broadcast_to([128, DC, TT]),
                    op=ALU.mult))
                rhs_ready = ("sV", vXN)
                tap(name + "_xn", xn[:, :], [128, DC * TT], BF16)

                b.wait(V, "sCF"); b.wait(V, "sTP")
                V.memset(gt[:, :], 0.0)
                b.wait(T, rhs_ready[0], rhs_ready[1])
                b.wait(T, "sPR", gw_tok)
                b.wait(T, "sA"); b.wait(T, "sV")
                ig = None
                for k in range(DC):
                    ig = T.matmul(pq8[0:8, 1536:2048], gwt_s[:, k * 8:(k + 1) * 8], rhs_xn(k),
                                  start=(k == 0), stop=(k == DC - 1))
                peg = b.inc("sPE", ig)
                b.wait(A_, "sPE", peg)
                aE = b.inc("sA", A_.activation(gt[0:8, 0:TT], pq8[0:8, 1536:2048],
                                               AF.Exp, scale=-1.0))
                b.wait(T, "sA", aE)
                it = None
                for c in range(4):
                    it = T.transpose(pq8[:, 1536 + c * 8:1544 + c * 8],
                                     gt[0:8, c * 128:(c + 1) * 128], identf[0:8, 0:8])
                pet = b.inc("sPE", it)
                b.wait(V, "sPE", pet)
                ga = gt[:, 512:1024]
                et = ga[:, 0:32]
                vser(V.tensor_copy(et, pq8[:, 1536:1568]))
                vser(V.tensor_scalar(ga[:, 32:64], et, 1.0, None, op0=ALU.add))
                vser(V.reciprocal(ga[:, 64:96], ga[:, 32:64]))
                sig = ga[:, 64:96]
                vser(V.tensor_tensor(ga[:, 96:128].rearrange("p (c e) -> p c e", e=8),
                                sig.rearrange("p (c e) -> p c e", e=8),
                                biasbc[:, :].unsqueeze(1).broadcast_to([128, 4, 8]),
                                op=ALU.add))
                r = ga[:, 96:128]
                vser(V.tensor_reduce(ga[:, 128:132], r.rearrange("p (c e) -> p c e", e=8),
                                axis=AX, op=ALU.max))
                vser(V.tensor_tensor(ga[:, 132:164].rearrange("p (c e) -> p c e", e=8),
                                r.rearrange("p (c e) -> p c e", e=8),
                                ga[:, 128:132].unsqueeze(2).broadcast_to([128, 4, 8]),
                                op=ALU.is_equal))
                mask1 = ga[:, 132:164]
                vser(V.scalar_tensor_tensor(ga[:, 164:196], mask1, -1e30, r,
                                       op0=ALU.mult, op1=ALU.add))
                r2 = ga[:, 164:196]
                vser(V.tensor_reduce(ga[:, 196:200], r2.rearrange("p (c e) -> p c e", e=8),
                                axis=AX, op=ALU.max))
                vser(V.tensor_tensor(ga[:, 200:232].rearrange("p (c e) -> p c e", e=8),
                                r2.rearrange("p (c e) -> p c e", e=8),
                                ga[:, 196:200].unsqueeze(2).broadcast_to([128, 4, 8]),
                                op=ALU.is_equal))
                mask2 = ga[:, 200:232]
                vser(V.tensor_tensor(ga[:, 232:264], mask1, mask2, op=ALU.add))
                vser(V.tensor_tensor(ga[:, 264:296], sig, ga[:, 232:264], op=ALU.mult))
                selg = ga[:, 264:296]
                vser(V.tensor_reduce(ga[:, 296:300], selg.rearrange("p (c e) -> p c e", e=8),
                                axis=AX, op=ALU.add))
                vser(V.reciprocal(ga[:, 300:304], ga[:, 296:300]))
                vC = b.inc("sV", V.scalar_tensor_tensor(
                    ga[:, 304:336].rearrange("p (c e) -> p c e", e=8),
                    selg.rearrange("p (c e) -> p c e", e=8), 0.5,
                    ga[:, 300:304].unsqueeze(2).broadcast_to([128, 4, 8]),
                    op0=ALU.mult, op1=ALU.mult))
                ct = ga[:, 304:336]
                tap(name + "_gt", gt[:, :], [128, 1024], F32)
                tap(name + "_ct", ct, [128, 32], F32)
                b.wait(T, "sV", vC)
                itc = None
                for c in range(4):
                    itc = T.transpose(pq8[0:8, 1536 + c * 128:1536 + (c + 1) * 128],
                                      ct[:, c * 8:(c + 1) * 8], identf[:, 0:128])
                pec = b.inc("sPE", itc)
                b.wait(V, "sPE", pec)
                cfmb = gt[0:8, 0:256].bitcast(BF16)
                vCf = b.inc("sV", V.tensor_copy(cfmb, pq8[0:8, 1536:2048]))
                b.wait(P, "sV", vCf)
                tcb = dma("sCF", P, rowscb[:, :], cfmb)

                def h1_evac(tg, pvw, pe_val):
                    (fi,) = tg
                    b.wait(A_, "sPE", pe_val)
                    return "sA", b.inc("sA", A_.activation(
                        h1s[:, fi * TT:(fi + 1) * TT], pvw, AF.Silu))

                def mk_h3_evac(dstbuf, coeff):
                    def f(tg, pvw, pe_val):
                        (fi,) = tg
                        b.wait(V, "sPE", pe_val)
                        b.wait(V, "sA")
                        return "sV", b.inc("sV", V.scalar_tensor_tensor(
                            dstbuf[:, fi * TT:(fi + 1) * TT], pvw, coeff,
                            h1s[:, fi * TT:(fi + 1) * TT], op0=ALU.mult, op1=ALU.mult))
                    return f

                def run_12(w1n, w3n, dstbuf, coeff):
                    g1 = mg_typeA(WSLICES[w1n], lambda m0, mw: (m0 // 128,))
                    g3 = mg_typeA(WSLICES[w3n], lambda m0, mw: (m0 // 128,))
                    ents = []
                    for ga_, gc in zip(g1, g3):
                        ents.append(dict(g=ga_, rhs=rhs_xn, evac=h1_evac, ready=rhs_ready))
                        ents.append(dict(g=gc, rhs=rhs_xn, evac=mk_h3_evac(dstbuf, coeff),
                                         ready=rhs_ready))
                    gemm(ents)

                def out_evac(tg, pvw, pe_val):
                    (mc,) = tg
                    b.wait(V, "sPE", pe_val)
                    return "sV", b.inc("sV", V.tensor_tensor(
                        xA[:, mc * TT:(mc + 1) * TT], xA[:, mc * TT:(mc + 1) * TT],
                        pvw, op=ALU.add))

                def run_2(w2lists, actbufs, ready2):
                    nexp = len(w2lists)
                    ents = []
                    for mbi in range(8):
                        gblocks = [w2lists[x][mbi] for x in range(nexp)]
                        ktiles = []
                        for x in range(nexp):
                            for k in range(FFC):
                                ktiles.append((x, k, (x, k)))
                        m0 = BLOCKS[gblocks[0]]["m0"]
                        g = dict(blocks=gblocks, ktiles=ktiles,
                                 msubs=[(0, 128, (m0 // 128,)), (128, 128, (m0 // 128 + 1,))])

                        def rhs2(kg, _bufs=actbufs):
                            x, k = kg
                            return _bufs[x][:, k * TT:(k + 1) * TT]
                        ents.append(dict(g=g, rhs=rhs2, evac=out_evac, ready=ready2))
                    gemm(ents)

                run_12("ws1t", "ws3t", act0, 0.5)
                tap(name + "_h1s", h1s[:, :], [128, FFC * TT], BF16)
                tap(name + "_act", act0[:, :], [128, FFC * TT], BF16)
                run_2([WSLICES["ws2t"]], [act0], ("sV", b.val("sV")))
                tap(name + "_xsh", xA[:, :], [128, DC * TT], F32)
                for p0 in range(0, NE, 2):
                    pump(name + ":pr%d" % p0)
                    pairbufs = [act0, act1]
                    last_pC = 0
                    for xi, e in enumerate((p0, p0 + 1)):
                        dst = pairbufs[xi]
                        run_12("we1t_%d" % e, "we3t_%d" % e, dst, 1.0)
                        b.wait(P, "sCF", tcb)
                        b.wait(P, "sV")
                        tb = dma("sCB", P, cbuf[:, xi * TT:(xi + 1) * TT],
                                 rowscb[e:e + 1, :].to_broadcast((128, TT)))
                        b.wait(V, "sCB", tb)
                        b.wait(V, "sPE")
                        last_pC = b.inc("sV", V.tensor_tensor(
                            dst[:, :].rearrange("p (f t) -> p f t", t=TT),
                            dst[:, :].rearrange("p (f t) -> p f t", t=TT),
                            cbuf[:, xi * TT:(xi + 1) * TT].unsqueeze(1)
                                .broadcast_to([128, FFC, TT]), op=ALU.mult))
                    run_2([WSLICES["we2t_%d" % p0], WSLICES["we2t_%d" % (p0 + 1)]],
                          pairbufs, ("sV", last_pC))
                b.x_ready = ("sV", b.val("sV"))
                tap(name + "_x", xA[:, :], [128, DC * TT], F32)

        sst = dict(csf_rd=None, csrow_rd=0)

        def ssd_chunk(c, xbc, dtf, dAf, csf3, csrow, ECSa, LLa, MPa, XTa, BTa):
            t0 = c * CH
            xr = xbc[:, :].rearrange("p (c w) -> p c w", w=XBCW)
            csf = csf3[:, 0:CH]
            dlf = csf3[:, CH:2 * CH]
            clf = csf3[:, 2 * CH:3 * CH]
            if sst["csf_rd"]:
                b.wait(V, sst["csf_rd"][0], sst["csf_rd"][1])
            b.wait(V, "sA")
            vCS = vser(V.tensor_tensor_scan(
                csf, dAf[:, t0:t0 + CH], zeros16[:, :], 0.0,
                op0=ALU.add, op1=ALU.add))
            b.wait(P, "sV", vCS)
            if sst["csrow_rd"]:
                b.wait(P, "sPE", sst["csrow_rd"])
            t0a = dma("sSD1", P, rowsc[5, :].rearrange("(h t) -> h t", t=CH), csf)
            sst["csf_rd"] = ("sSD1", t0a)
            b.wait(P, "sSD1", t0a)
            t1 = dma("sSD1", P, csrow[:, :], rowsc[5:6, 0:NH * CH])
            V.tensor_scalar(dlf, csf, csf[:, 127:128], None, op0=ALU.subtract)
            vDL = b.inc("sV", V.tensor_scalar(clf, csf, 0.0, csf[:, 127:128],
                                              op0=ALU.mult, op1=ALU.add))
            b.wait(T, "sV", vDL)
            T.transpose(pq8[:, 512:528], csf, identf[0:16, 0:16])
            T.transpose(pq8[:, 528:544], dtf[:, t0:t0 + CH], identf[0:16, 0:16])
            T.transpose(pq8[:, 544:560], dlf, identf[0:16, 0:16])
            iT = T.transpose(pq8[:, 560:576], clf, identf[0:16, 0:16])
            peT = b.inc("sPE", iT)
            b.wait(V, "sPE", peT)
            vser(V.tensor_copy(smf[:, 0:16], pq8[:, 512:528]))
            vT = vser(V.tensor_copy(smf[:, 16:32], pq8[:, 528:544]))
            b.wait(A_, "sPE", peT)
            b.wait(A_, "sV")
            A_.activation(smf[:, 48:64], pq8[:, 544:560], AF.Exp, scale=-1.0)
            aE2 = b.inc("sA", A_.activation(smf[:, 64:80], pq8[:, 560:576], AF.Exp))
            for s in range(4):
                sem, v = pfence[s]
                b.wait(T, sem, v)
            b.wait(T, "sSD1", t1)
            peCS = bcast_mm(csrow, pquad[:, :], ("sSD1", t1))
            sst["csrow_rd"] = peCS
            b.wait(V, "sPE", peCS)
            V.scalar_tensor_tensor(
                LLa[:, :].rearrange("p (h t) -> p h t", t=CH),
                smf[:, 0:16].unsqueeze(2).broadcast_to([128, NH, CH]), -1.0,
                pquad[:, :].rearrange("p (h t) -> p h t", t=CH),
                op0=ALU.mult, op1=ALU.add)
            vLL = b.inc("sV", V.tensor_tensor(
                LLa[:, :].rearrange("p (h t) -> p h t", t=CH),
                LLa[:, :].rearrange("p (h t) -> p h t", t=CH),
                maskadd[:, :].unsqueeze(1).broadcast_to([128, NH, CH]), op=ALU.add))
            b.wait(A_, "sV", vLL)
            aLE = b.inc("sA", A_.activation(LLa[:, :], LLa[:, :], AF.Exp))
            aEC = b.inc("sA", A_.activation(ECSa[:, :], pquad[:, :], AF.Exp))
            b.wait(T, "sV", vLL)
            b.wait(T, "sA")
            ig = None
            for h in range(NH):
                g = h // 2
                ig = T.matmul(pquad[:, h * CH:(h + 1) * CH],
                              xbc[:, (8 + g) * XBCW + 3 + t0:(8 + g) * XBCW + 3 + t0 + CH],
                              xbc[:, (16 + h) * XBCW + 3 + t0:(16 + h) * XBCW + 3 + t0 + CH],
                              start=True, stop=True)
            peG = b.inc("sPE", ig)
            ix = None
            for g in range(NG):
                ix = T.transpose(ptp[:, g * CH:(g + 1) * CH],
                                 xbc[:, g * XBCW + 3 + t0:g * XBCW + 3 + t0 + CH],
                                 identb[:, :])
            peX = b.inc("sPE", ix)
            b.wait(V, "sPE", peG)
            b.wait(V, "sA", aLE)
            vMP = b.inc("sV", V.tensor_tensor(MPa[:, :], pquad[:, :], LLa[:, :], op=ALU.mult))
            b.wait(V, "sPE", peX)
            vXT = b.inc("sV", V.tensor_tensor(
                XTa[:, :].rearrange("p (g r t) -> p g r t", r=2, t=CH),
                ptp[:, :].rearrange("p (g t) -> p g t", t=CH).unsqueeze(2)
                    .broadcast_to([128, NG, 2, CH]),
                smf[:, 16:32].rearrange("p (g r) -> p g r", r=2).unsqueeze(3)
                    .broadcast_to([128, NG, 2, CH]), op=ALU.mult))
            b.wait(T, "sV", vXT)
            ib = None
            for g in range(NG):
                ib = T.transpose(ptp[:, g * CH:(g + 1) * CH],
                                 xbc[:, (8 + g) * XBCW + 3 + t0:(8 + g) * XBCW + 3 + t0 + CH],
                                 identb[:, :])
            peB = b.inc("sPE", ib)
            b.wait(V, "sPE", peB)
            b.wait(V, "sA", aE2)
            vBT = b.inc("sV", V.tensor_tensor(
                BTa[:, :].rearrange("p (g r t) -> p g r t", r=2, t=CH),
                ptp[:, :].rearrange("p (g t) -> p g t", t=CH).unsqueeze(2)
                    .broadcast_to([128, NG, 2, CH]),
                smf[:, 48:64].rearrange("p (g r) -> p g r", r=2).unsqueeze(3)
                    .broadcast_to([128, NG, 2, CH]), op=ALU.mult))
            b.wait(P, "sA", aEC)
            b.wait(P, "sPE", peG)
            b.wait(P, "sV")
            pCT = b.inc("sP", P.tensor_tensor(
                ECSa[:, :].rearrange("p (h t) -> p h t", t=CH),
                ECSa[:, :].rearrange("p (h t) -> p h t", t=CH),
                xr[:, 16:32, 3 + t0:3 + t0 + CH], op=ALU.mult))
            pSB = b.inc("sP", P.tensor_copy(Stb[:, :], St[:, :]))
            b.wait(T, "sV", vMP)
            b.wait(T, "sV", vBT)
            b.wait(T, "sP", pSB)
            iy = None
            for h in range(NH):
                T.matmul(pquad[:, h * CH:(h + 1) * CH], XTa[:, h * CH:(h + 1) * CH],
                         MPa[:, h * CH:(h + 1) * CH], start=True, stop=False)
                iy = T.matmul(pquad[:, h * CH:(h + 1) * CH], Stb[:, h * CH:(h + 1) * CH],
                              ECSa[:, h * CH:(h + 1) * CH], start=False, stop=True)
            peY = b.inc("sPE", iy)
            b.wait(V, "sPE", peY)
            V.tensor_tensor(MPa[:, :].rearrange("p (g r t) -> p g r t", r=2, t=CH),
                            xr[:, 0:8, 3 + t0:3 + t0 + CH].unsqueeze(2)
                            .broadcast_to([128, NG, 2, CH]),
                            Dbc[:, :].rearrange("p (g r) -> p g r", r=2).unsqueeze(3)
                            .broadcast_to([128, NG, 2, CH]), op=ALU.mult)
            vY = b.inc("sV", V.tensor_tensor(
                yb[:, :].rearrange("p (hc t) -> p hc t", t=TT)[:, :, t0:t0 + CH],
                MPa[:, :].rearrange("p (h t) -> p h t", t=CH),
                pquad[:, :].rearrange("p (h t) -> p h t", t=CH), op=ALU.add))
            for s in range(4):
                pfence[s] = ("sV", vY)
            for half in range(2):
                b.wait(T, "sV")
                b.wait(T, "sA", aEC)
                is_ = None
                for hh in range(8):
                    h = half * 8 + hh
                    is_ = T.matmul(pq8[:, 1024 + hh * CH:1024 + (hh + 1) * CH],
                                   BTa[:, h * CH:(h + 1) * CH], XTa[:, h * CH:(h + 1) * CH],
                                   start=True, stop=True)
                peS = b.inc("sPE", is_)
                b.wait(V, "sPE", peS)
                b.wait(V, "sP", pSB)
                V.tensor_tensor(
                    St[:, half * 1024:(half + 1) * 1024].rearrange("p (g t) -> p g t", t=CH),
                    St[:, half * 1024:(half + 1) * 1024].rearrange("p (g t) -> p g t", t=CH),
                    smf[:, 64 + half * 8:72 + half * 8].unsqueeze(2)
                        .broadcast_to([128, 8, CH]), op=ALU.mult)
                b.inc("sV", V.tensor_tensor(
                    St[:, half * 1024:(half + 1) * 1024],
                    St[:, half * 1024:(half + 1) * 1024],
                    pq8[:, 1024:2048], op=ALU.add))

        def attn_call(tile_in_seq):
            with nc.sbuf_tensor(uname("a_res"), [128, DC * TT], BF16) as resb, \
                 nc.sbuf_tensor(uname("a_u0"), [1, 1], F32) as _unused0:
                with nc.sbuf_tensor(uname("a_xbf"), [128, DC * TT], BF16) as xbf, \
                     nc.sbuf_tensor(uname("a_xsq"), [128, DC * TT], BF16) as xsq:
                    rows_rdy, _ = ln_stats(xA, xbf, xsq)
                    b.wait(A_, rows_rdy[0], rows_rdy[1])
                    aL = aser(A_.activation(row(2), row(2), AF.Ln))
                    b.wait(V, "sA", aL)
                    vq = b.inc("sV", V.tensor_tensor(row(2), row(2), row(3), op=ALU.subtract))
                    b.wait(A_, "sV", vq)
                    aser(A_.activation(row(2), row(2), AF.Exp))
                    aser(A_.activation(row(2), row(2), AF.Ln, bias=EPS))
                    aq = b.inc("sA", A_.activation(row(2), row(2), AF.Exp, scale=-0.5))
                    b.wait(T, "sV"); b.wait(T, "sA")
                    pm = bcast_mm(row(0), pq8[:, 0:512], rows_rdy)
                    pr = bcast_mm(row(1), pq8[:, 1536:2048], rows_rdy)
                    p2 = bcast_mm(row(2), pq8[:, 512:1024], ("sA", aq))
                    sst["csrow_rd"] = p2
                    b.wait(V, "sPE", pr)
                    V.tensor_tensor(xbf[:, :].rearrange("p (c t) -> p c t", t=TT),
                                    xA[:, :].rearrange("p (c t) -> p c t", t=TT),
                                    pq8[:, 0:512].unsqueeze(1).broadcast_to([128, DC, TT]),
                                    op=ALU.subtract)
                    vRES = b.inc("sV", V.tensor_tensor(
                        resb[:, :].rearrange("p (c t) -> p c t", t=TT),
                        xbf[:, :].rearrange("p (c t) -> p c t", t=TT),
                        pq8[:, 1536:2048].unsqueeze(1).broadcast_to([128, DC, TT]),
                        op=ALU.mult))
                    b.wait(V, "sPE", p2)
                    V.tensor_tensor(xbf[:, :].rearrange("p (c t) -> p c t", t=TT),
                                    resb[:, :].rearrange("p (c t) -> p c t", t=TT),
                                    pq8[:, 512:1024].unsqueeze(1).broadcast_to([128, DC, TT]),
                                    op=ALU.mult)
                    vXN = b.inc("sV", V.tensor_tensor(
                        xn[:, :].rearrange("p (c t) -> p c t", t=TT),
                        xbf[:, :].rearrange("p (c t) -> p c t", t=TT),
                        lnw["ln1wc"][:, :].unsqueeze(2).broadcast_to([128, DC, TT]),
                        op=ALU.mult))
                rhs_ready = ("sV", vXN)
                tap("attn_xn", xn[:, :], [128, DC * TT], BF16)
                tap("attn_res", resb[:, :], [128, DC * TT], BF16)

                mixer_es = ExitStack()
                xbc = mixer_es.enter_context(nc.sbuf_tensor(uname("a_xbc"), [128, 32 * XBCW], BF16))
                dtf = rowar[0:16, 0:TT]
                dAf = rowar[0:16, TT:2 * TT]
                u16 = rowar[0:16, 2 * TT:3 * TT]
                cvb = mixer_es.enter_context(nc.sbuf_tensor(uname("a_cv"), [128, 3 * TT], BF16))
                xr = xbc[:, :].rearrange("p (c w) -> p c w", w=XBCW)
                silu_tok = {}

                def conv_ch(ch):
                    base = ch * XBCW
                    sl = ch % 3
                    tmp = cvb[:, sl * TT:(sl + 1) * TT]
                    if sl in silu_tok:
                        b.wait(V, "sA", silu_tok[sl])
                    if tile_in_seq == 0:
                        V.memset(xbc[:, base:base + 3], 0.0)
                    else:
                        V.tensor_copy(xbc[:, base:base + 3], tailb[:, ch * 3:ch * 3 + 3])
                    ins = V.scalar_tensor_tensor(
                        tmp, xbc[:, base + 0:base + TT],
                        cwt[:, ch * 4:ch * 4 + 1],
                        cbv[:, ch:ch + 1].broadcast_to([128, TT]),
                        op0=ALU.mult, op1=ALU.add)
                    for k in (1, 2, 3):
                        ins = V.scalar_tensor_tensor(
                            tmp, xbc[:, base + k:base + k + TT],
                            cwt[:, ch * 4 + k:ch * 4 + k + 1], tmp,
                            op0=ALU.mult, op1=ALU.add)
                    tok = b.inc("sV", V.tensor_copy(tailb[:, ch * 3:ch * 3 + 3],
                                                    xbc[:, base + TT:base + TT + 3]))
                    b.wait(A_, "sV", tok)
                    silu_tok[sl] = b.inc("sA", A_.activation(
                        xbc[:, base + 3:base + 3 + TT], tmp, AF.Silu))

                def proj_evac(tg, pvw, pe_val):
                    kind, idx = tg
                    if kind == "xbc":
                        b.wait(A_, "sPE", pe_val)
                        v = b.inc("sA", A_.copy(xbc[:, idx * XBCW + 3: idx * XBCW + 3 + TT], pvw))
                        b.wait(V, "sA", v)
                        conv_ch(idx)
                        return "sA", v
                    else:
                        b.wait(A_, "sPE", pe_val)
                        aser(A_.activation(u16[:, :], pvw, AF.Abs, bias=dtb[:, 0:1]))
                        aser(A_.activation(u16[:, :], u16[:, :], AF.Exp, scale=-1.0))
                        aser(A_.activation(u16[:, :], u16[:, :], AF.Ln, bias=1.0))
                        aI = b.inc("sA", A_.activation(dtf[:, :], pvw, AF.Relu, bias=dtb[:, 0:1]))
                        b.wait(V, "sA", aI)
                        vser(V.tensor_tensor(dtf[:, :], dtf[:, :], u16[:, :], op=ALU.add))
                        v = b.inc("sV", V.tensor_scalar(dAf[:, :], dtf[:, :], aneg[:, 0:1],
                                                        None, op0=ALU.mult))
                        return "sV", v

                win = WSLICES["w_in"]
                ents = [dict(g=g_, rhs=rhs_xn, evac=proj_evac, ready=rhs_ready)
                        for g_ in mg_typeA(win[8:24],
                                           lambda m0, mw: ("xbc", (m0 - 2048) // 128))]
                ents += [dict(g=g_, rhs=rhs_xn, evac=proj_evac, ready=rhs_ready)
                         for g_ in mg_typeA(win[24:25], lambda m0, mw: ("dt", 0))]
                gemm(ents)
                pump("attn:proj")
                tap("attn_dt", dtf[:, :], [16, TT], F32)

                if tile_in_seq == 0:
                    b.wait(V, "sPE"); b.wait(V, "sP")
                    b.inc("sV", V.memset(St[:, :], 0.0))
                with nc.sbuf_tensor(uname("a_ecs"), [128, NH * CH], BF16) as ECSa, \
                     nc.sbuf_tensor(uname("a_ll"), [128, NH * CH], F32) as LLa, \
                     nc.sbuf_tensor(uname("a_mp"), [128, NH * CH], BF16) as MPa, \
                     nc.sbuf_tensor(uname("a_xt"), [128, NH * CH], BF16) as XTa, \
                     nc.sbuf_tensor(uname("a_bt"), [128, NH * CH], BF16) as BTa, \
                     nc.sbuf_tensor(uname("a_u0d"), [1, 1], F32) as _u0d:
                    csf3 = rowar[0:16, 3 * TT:3 * TT + 3 * CH]
                    csrow = rowar[32:33, 0:NH * CH]
                    for c in range(NCH):
                        ssd_chunk(c, xbc, dtf, dAf, csf3, csrow, ECSa, LLa, MPa, XTa, BTa)
                pump("attn:ssd")
                tap("attn_y", yb[:, :], [128, DC * TT], BF16)
                mixer_es.close()

                gate_es = ExitStack()
                ynb = gate_es.enter_context(nc.sbuf_tensor(uname("a_yn"), [128, DC * TT], BF16))
                res2 = gate_es.enter_context(nc.sbuf_tensor(uname("a_res2"), [128, DC * TT], F32))
                ztmp = gate_es.enter_context(nc.sbuf_tensor(uname("a_zt"), [128, 2 * TT], BF16))
                if True:
                    zslot = [0]

                    def z_evac(tg, pvw, pe_val):
                        (mc,) = tg
                        sl = zslot[0]; zslot[0] = 1 - sl
                        b.wait(A_, "sPE", pe_val)
                        b.wait(A_, "sV")
                        aZ = b.inc("sA", A_.activation(
                            ztmp[:, sl * TT:(sl + 1) * TT], pvw, AF.Silu))
                        b.wait(V, "sA", aZ)
                        vZ = b.inc("sV", V.tensor_tensor(
                            yb[:, mc * TT:(mc + 1) * TT], yb[:, mc * TT:(mc + 1) * TT],
                            ztmp[:, sl * TT:(sl + 1) * TT], op=ALU.mult))
                        return "sV", vZ
                    gemm([dict(g=g_, rhs=rhs_xn, evac=z_evac, ready=rhs_ready)
                          for g_ in mg_typeA(win[0:8], lambda m0, mw: (m0 // 128,))])
                    vyg = b.val("sV")
                    b.wait(P, "sV", vyg)
                    pq = b.inc("sP", P.tensor_tensor(ynb[:, :], yb[:, :], yb[:, :],
                                                     op=ALU.mult))
                    b.wait(T, "sP", pq)
                    b.wait(T, "sA")
                    i2 = None
                    for k in range(DC):
                        i2 = T.matmul(pq8[0:1, 512:1024], onesb[:],
                                      ynb[:, k * TT:(k + 1) * TT],
                                      start=(k == 0), stop=(k == DC - 1))
                    pe2 = b.inc("sPE", i2)
                    b.wait(A_, "sPE", pe2)
                    aser(A_.activation(row(3), pq8[0:1, 512:1024], AF.Ln, bias=EPS, scale=1.0 / D))
                    aR = b.inc("sA", A_.activation(row(1), row(3), AF.Exp, scale=-0.5))
                    b.wait(T, "sV"); b.wait(T, "sA")
                    pr2 = bcast_mm(row(1), pq8[:, 1536:2048], ("sA", aR))
                    b.wait(V, "sPE", pr2)
                    V.tensor_tensor(ynb[:, :].rearrange("p (c t) -> p c t", t=TT),
                                    yb[:, :].rearrange("p (c t) -> p c t", t=TT),
                                    pq8[:, 1536:2048].unsqueeze(1).broadcast_to([128, DC, TT]),
                                    op=ALU.mult)
                    vYN = b.inc("sV", V.tensor_tensor(
                        ynb[:, :].rearrange("p (c t) -> p c t", t=TT),
                        ynb[:, :].rearrange("p (c t) -> p c t", t=TT),
                        lnw["normwc"][:, :].unsqueeze(2).broadcast_to([128, DC, TT]),
                        op=ALU.mult))
                    tap("attn_yn", ynb[:, :], [128, DC * TT], BF16)

                    def wo_evac(tg, pvw, pe_val):
                        (mc,) = tg
                        b.wait(V, "sPE", pe_val)
                        return "sV", b.inc("sV", V.tensor_tensor(
                            res2[:, mc * TT:(mc + 1) * TT], resb[:, mc * TT:(mc + 1) * TT],
                            pvw, op=ALU.add))

                    def rhs_yn(k):
                        return ynb[:, k * TT:(k + 1) * TT]
                    gemm([dict(g=g_, rhs=rhs_yn, evac=wo_evac, ready=("sV", vYN))
                          for g_ in mg_typeA(WSLICES["w_out"], lambda m0, mw: (m0 // 128,))])
                    tap("attn_res2", res2[:, :], [128, DC * TT], F32)
                    b.wait(P, "sV"); b.wait(P, "sPE")
                    pX = b.inc("sP", P.tensor_tensor(xA[:, :], xA[:, :], res2[:, :],
                                                     op=ALU.add))
                    b.wait(A_, "sV"); b.wait(A_, "sP", pq)
                    aSq = b.inc("sA", A_.activation(yb[:, :], res2[:, :], AF.Square))
                    b.wait(T, "sA", aSq)
                    i3 = None
                    for k in range(DC):
                        i3 = T.matmul(pq8[0:1, 1024:1536], onesb[:],
                                      yb[:, k * TT:(k + 1) * TT],
                                      start=(k == 0), stop=(k == DC - 1))
                    pe3 = b.inc("sPE", i3)
                    b.wait(A_, "sPE", pe3)
                    aser(A_.activation(row(3), pq8[0:1, 1024:1536], AF.Ln, bias=EPS, scale=1.0 / D))
                    aR2 = b.inc("sA", A_.activation(row(1), row(3), AF.Exp, scale=-0.5))
                    b.wait(T, "sV")
                    pr3 = bcast_mm(row(1), pq8[:, 1536:2048], ("sA", aR2))
                    b.wait(V, "sPE", pr3)
                    V.tensor_tensor(yb[:, :].rearrange("p (c t) -> p c t", t=TT),
                                    res2[:, :].rearrange("p (c t) -> p c t", t=TT),
                                    pq8[:, 1536:2048].unsqueeze(1).broadcast_to([128, DC, TT]),
                                    op=ALU.mult)
                    vXN2 = b.inc("sV", V.tensor_tensor(
                        xn[:, :].rearrange("p (c t) -> p c t", t=TT),
                        yb[:, :].rearrange("p (c t) -> p c t", t=TT),
                        lnw["ln2wc"][:, :].unsqueeze(2).broadcast_to([128, DC, TT]),
                        op=ALU.mult))
                    rhs2_ready = ("sV", vXN2)
                    tap("attn_xn2", xn[:, :], [128, DC * TT], BF16)
                gate_es.close()

                with nc.sbuf_tensor(uname("a_act"), [128, 64 * TT], BF16) as actI:
                    def ih1_evac(tg, pvw, pe_val):
                        (fi,) = tg
                        b.wait(A_, "sPE", pe_val)
                        return "sA", b.inc("sA", A_.activation(
                            actI[:, fi * TT:(fi + 1) * TT], pvw, AF.Silu))

                    def ih3_evac(tg, pvw, pe_val):
                        (fi,) = tg
                        b.wait(V, "sPE", pe_val)
                        b.wait(V, "sA")
                        return "sV", b.inc("sV", V.tensor_tensor(
                            actI[:, fi * TT:(fi + 1) * TT], pvw,
                            actI[:, fi * TT:(fi + 1) * TT], op=ALU.mult))

                    g1 = mg_typeA(WSLICES["w1"], lambda m0, mw: (m0 // 128,))
                    g3 = mg_typeA(WSLICES["w3"], lambda m0, mw: (m0 // 128,))
                    ents = []
                    for ga_, gc in zip(g1, g3):
                        ents.append(dict(g=ga_, rhs=rhs_xn, evac=ih1_evac, ready=rhs2_ready))
                        ents.append(dict(g=gc, rhs=rhs_xn, evac=ih3_evac, ready=rhs2_ready))
                    gemm(ents)

                    def iout_evac(tg, pvw, pe_val):
                        (mc,) = tg
                        b.wait(V, "sPE", pe_val)
                        b.wait(V, "sP")
                        return "sV", b.inc("sV", V.tensor_tensor(
                            xA[:, mc * TT:(mc + 1) * TT], xA[:, mc * TT:(mc + 1) * TT],
                            pvw, op=ALU.add))

                    def rhs_act(kg):
                        return actI[:, kg * TT:(kg + 1) * TT]
                    w2l = WSLICES["w2"]
                    ents2 = []
                    for mc in range(16):
                        bA, bB = w2l[2 * mc], w2l[2 * mc + 1]
                        ktiles = ([(0, k, k) for k in range(32)] +
                                  [(1, k, 32 + k) for k in range(32)])
                        ents2.append(dict(g=dict(blocks=[bA, bB], msubs=[(0, 128, (mc,))],
                                                 ktiles=ktiles),
                                          rhs=rhs_act, evac=iout_evac,
                                          ready=("sV", b.val("sV"))))
                    gemm(ents2)
                    b.x_ready = ("sV", b.val("sV"))
                tap("attn_out", xA[:, :], [128, DC * TT], F32)

        out_fence = 0
        for ti in range(N_TILES):
            s_i = ti // TPS
            j_i = ti % TPS
            tok0 = j_i * TT
            b.wait(P, "sXO", out_fence)
            for sn in ("sV", "sA", "sPE", "sP"):
                b.wait(P, sn)
            tin = dma("sXI", P,
                      xA[:, :].rearrange("p (c t) -> p c t", t=TT),
                      dr["xin"][s_i, :, tok0:tok0 + TT].rearrange("(c p) t -> p c t", p=128))
            b.x_ready = ("sXI", tin)
            for e_ in (V, A_, P, T):
                b.wait(e_, "sXI", tin)
            tap("t%d_xin" % ti, xA[:, :], [128, DC * TT], F32)
            ffn_call("pre%d" % ti)
            tap("t%d_x1" % ti, xA[:, :], [128, DC * TT], F32)
            attn_call(j_i)
            tap("t%d_x2" % ti, xA[:, :], [128, DC * TT], F32)
            ffn_call("post%d" % ti)
            for sn in ("sV", "sA", "sP", "sPE"):
                b.wait(P, sn)
            out_fence = dma("sXO", P,
                            dr["xout"][s_i, :, tok0:tok0 + TT].rearrange("(c p) t -> p c t", p=128),
                            xA[:, :].rearrange("p (c t) -> p c t", t=TT))
        b.wait(P, "sXO")
        b.wait(P, "sTP")
    es.close()
    return nc, b


def _host_prep(inputs):
    g = {}
    f = np.float32
    g["w_in"] = np.ascontiguousarray(np.asarray(inputs["L0_attn_W_in"], dtype=f))
    g["w_out"] = np.ascontiguousarray(np.asarray(inputs["L0_attn_W_out"], dtype=f))
    g["w1"] = np.ascontiguousarray(np.asarray(inputs["L0_attn_w1"], dtype=f))
    g["w3"] = np.ascontiguousarray(np.asarray(inputs["L0_attn_w3"], dtype=f))
    g["w2"] = np.ascontiguousarray(np.asarray(inputs["L0_attn_w2"], dtype=f))
    g["ws1t"] = np.ascontiguousarray(np.asarray(inputs["L0_ffn_ws1"], dtype=f).T)
    g["ws3t"] = np.ascontiguousarray(np.asarray(inputs["L0_ffn_ws3"], dtype=f).T)
    g["ws2t"] = np.ascontiguousarray(np.asarray(inputs["L0_ffn_ws2"], dtype=f).T)
    g["gwt"] = np.ascontiguousarray(np.asarray(inputs["L0_ffn_gate_w"], dtype=f).T)
    g["we1t"] = np.ascontiguousarray(np.asarray(inputs["L0_ffn_we1"], dtype=f).transpose(0, 2, 1))
    g["we3t"] = np.ascontiguousarray(np.asarray(inputs["L0_ffn_we3"], dtype=f).transpose(0, 2, 1))
    g["we2t"] = np.ascontiguousarray(np.asarray(inputs["L0_ffn_we2"], dtype=f).transpose(0, 2, 1))
    g["ln1wc"] = np.ascontiguousarray(np.asarray(inputs["L0_attn_ln1_w"], dtype=f).reshape(16, 128).T)
    g["normwc"] = np.ascontiguousarray(np.asarray(inputs["L0_attn_norm_w"], dtype=f).reshape(16, 128).T)
    g["ln2wc"] = np.ascontiguousarray(np.asarray(inputs["L0_attn_ln2_w"], dtype=f).reshape(16, 128).T)
    cw = np.asarray(inputs["L0_attn_conv_w"], dtype=f)
    g["cwt"] = np.ascontiguousarray(cw.T.reshape(32, 128, 4).transpose(1, 0, 2).reshape(128, 128))
    g["cb"] = np.ascontiguousarray(np.asarray(inputs["L0_attn_conv_b"], dtype=f).reshape(32, 128).T)
    g["dtb"] = np.ascontiguousarray(np.asarray(inputs["L0_attn_dt_bias"], dtype=f).reshape(16, 1))
    g["alog"] = np.ascontiguousarray(np.asarray(inputs["L0_attn_A_log"], dtype=f).reshape(16, 1))
    g["drow"] = np.ascontiguousarray(np.asarray(inputs["L0_attn_D"], dtype=f).reshape(1, 16))
    g["biasrow"] = np.ascontiguousarray(np.asarray(inputs["L0_ffn_bias"], dtype=f).reshape(1, 8))
    g["identf"] = np.eye(128, dtype=f)
    g["maskadd"] = np.where(np.arange(128)[:, None] <= np.arange(128)[None, :],
                            f(0), f(-1e30)).astype(f)
    return g


def kernel(**inputs):
    try:
        from concourse.compiler_utils import get_compiler_flags, set_compiler_flags
        fl = [f.replace("--enable-ldw-opt=false", "--enable-ldw-opt=true")
              for f in get_compiler_flags()]
        set_compiler_flags(fl)
    except Exception:
        pass
    x = np.asarray(inputs["L0_attn_hidden_states"], dtype=np.float32)
    g = _host_prep(inputs)
    nc, b = build_nc()
    in_maps = []
    for c in range(8):
        m = dict(g)
        m["xin"] = np.ascontiguousarray(x[2 * c:2 * c + 2].transpose(0, 2, 1))
        in_maps.append(m)
    res = run_bass_kernel_spmd(nc, in_maps, core_ids=list(range(8)))
    LAST_RESULTS[0] = res
    outs = []
    for c in range(8):
        o = np.asarray(res.results[c]["xout"])
        outs.append(np.ascontiguousarray(o.transpose(0, 2, 1)))
    return np.concatenate(outs, axis=0)
